# version 20
# baseline (speedup 1.0000x reference)
import contextlib
import numpy as np

import concourse.bass as bass
import concourse.mybir as mybir

F32 = mybir.dt.float32
BF16 = mybir.dt.bfloat16
I32 = mybir.dt.int32
AF = mybir.ActivationFunctionType
ALU = mybir.AluOpType
AX = mybir.AxisListType

S = 2048
D = 2048
IF = 8192
NSB = 16
NHB = 16
NIB = 64
H = 16
KVH = 4
HD = 128
EPS_LN = 1e-5
EPS_RMS = 1.1920928955078125e-07
INVSQRT_HD = 1.0 / float(np.sqrt(128.0))

ENGS = ("tensor", "vector", "scalar", "sync", "gpsimd")


class Script:
    def __init__(self):
        self.items = []
        self.sem_count = {}

    def _sem(self, key):
        self.sem_count.setdefault(key, 0)
        return key

    def op(self, eng, fn, inc=None, inc_amt=1):
        val = None
        if inc is not None:
            self._sem(inc)
            self.sem_count[inc] += inc_amt
            val = self.sem_count[inc]
        self.items.append((eng, "op", (fn, inc, inc_amt)))
        return val

    def dma(self, eng, out, in_, inc, **kw):
        self._sem(inc)
        if eng == "gpsimd":
            self.wait(eng, inc, self.sem_count[inc])
        self.sem_count[inc] += 16
        val = self.sem_count[inc]

        def fn(h, out=out, in_=in_, kw=kw):
            return h.dma_start(out=out, in_=in_, **kw)

        self.items.append((eng, "op", (fn, inc, 16)))
        return val

    def wait(self, eng, key, val):
        if val is None or (isinstance(val, int) and val <= 0):
            return
        self._sem(key)
        self.items.append((eng, "wait", (key, val)))

    def barrier(self, engines=ENGS, exclude=()):
        snap = dict(self.sem_count)
        for e in engines:
            for key, val in snap.items():
                if val > 0 and key not in exclude:
                    self.items.append((e, "wait", (key, val)))

    def current(self, key):
        return self.sem_count.get(key, 0)

    def fence(self, eng, key):
        self.wait(eng, key, self.current(key))

    def check(self):
        queues = {e: [] for e in ENGS}
        for eng, kind, payload in self.items:
            queues[eng].append((kind, payload))
        pos = {e: 0 for e in ENGS}
        sems = {k: 0 for k in self.sem_count}
        progress = True
        while progress:
            progress = False
            for e in ENGS:
                while pos[e] < len(queues[e]):
                    kind, payload = queues[e][pos[e]]
                    if kind == "wait":
                        key, val = payload
                        if sems.get(key, 0) < val:
                            break
                    else:
                        _, inc, amt = payload
                        if inc is not None:
                            sems[inc] += amt
                    pos[e] += 1
                    progress = True
        stuck = {e: (pos[e], len(queues[e])) for e in ENGS if pos[e] < len(queues[e])}
        if stuck:
            det = []
            for e, (p, n) in stuck.items():
                kind, payload = queues[e][p]
                det.append(f"{e}@{p}/{n}:{kind}:{payload if kind == 'wait' else ''}")
            raise RuntimeError(f"script deadlock: {det}")

    def emit(self, blk, sem_handles):
        def make_body(eng):
            def body(h):
                for e, kind, payload in self.items:
                    if e != eng:
                        continue
                    if kind == "wait":
                        key, val = payload
                        h.wait_ge(sem_handles[key], val)
                    else:
                        fn, inc, amt = payload
                        ins = fn(h)
                        if inc is not None:
                            ins.then_inc(sem_handles[inc], amt)
            return body

        blk.tensor(make_body("tensor"))
        blk.vector(make_body("vector"))
        blk.scalar(make_body("scalar"))
        blk.sync(make_body("sync"))
        blk.gpsimd(make_body("gpsimd"))


def build_program(parts=("f1", "at", "f2")):
    nc = bass.Bass("TRN2", target_bir_lowering=False)

    x0 = nc.dram_tensor("x0", [S, D], F32, kind="ExternalInput").ap()
    wq_t = nc.dram_tensor("wq_t", [D, D], F32, kind="ExternalInput").ap()
    wk_t = nc.dram_tensor("wk_t", [D, KVH * HD], F32, kind="ExternalInput").ap()
    wv_t = nc.dram_tensor("wv_t", [D, KVH * HD], F32, kind="ExternalInput").ap()
    wo_t = nc.dram_tensor("wo_t", [D, D], F32, kind="ExternalInput").ap()
    q_gain = nc.dram_tensor("q_gain", [1, H], F32, kind="ExternalInput").ap()
    gw_t = nc.dram_tensor("gw_t", [D, IF], I32, kind="ExternalInput").ap()
    uw_t = nc.dram_tensor("uw_t", [D, IF], I32, kind="ExternalInput").ap()
    dw_t = nc.dram_tensor("dw_t", [IF, D], I32, kind="ExternalInput").ap()
    g_sc = nc.dram_tensor("g_sc", [128, NIB], F32, kind="ExternalInput").ap()
    u_sc = nc.dram_tensor("u_sc", [128, NIB], F32, kind="ExternalInput").ap()
    d_sc = nc.dram_tensor("d_sc", [1, D], F32, kind="ExternalInput").ap()
    c_ident = nc.dram_tensor("c_ident", [128, 128], F32, kind="ExternalInput").ap()
    c_cos = nc.dram_tensor("c_cos", [S, H, 64], F32, kind="ExternalInput").ap()
    c_sin = nc.dram_tensor("c_sin", [S, H, 64], F32, kind="ExternalInput").ap()
    c_mask = nc.dram_tensor("c_mask", [128, 640], F32, kind="ExternalInput").ap()

    out = nc.dram_tensor("out", [S, D], F32, kind="ExternalOutput").ap()

    x1_sc = nc.dram_tensor("x1_sc", [S, D], F32, kind="Internal").ap()
    x2_sc = nc.dram_tensor("x2_sc", [S, D], F32, kind="Internal").ap()
    gw_sc = nc.dram_tensor("gw_sc", [NIB, 128, D], BF16, kind="Internal").ap()
    uw_sc = nc.dram_tensor("uw_sc", [NIB, 128, D], BF16, kind="Internal").ap()
    dw_sc = nc.dram_tensor("dw_sc", [NIB, 128, D], BF16, kind="Internal").ap()
    wo_sc = nc.dram_tensor("wo_sc", [NHB, 128, D], BF16, kind="Internal").ap()
    h2_sc = nc.dram_tensor("h2_sc", [NIB, 128, S], BF16, kind="Internal").ap()

    es = contextlib.ExitStack()
    with es:
        KB = 1024

        def alloc(name, shape, dt):
            return nc.alloc_sbuf_tensor(name, shape, dt)

        xt = alloc("xt", [128, 2, D], F32)
        res_f = alloc("res_f", [128, 2, D], F32)
        ident = alloc("ident", [128, 128], F32)
        ones1 = alloc("ones1", [1, 128], F32)
        gainrow = alloc("gainrow", [1, H], F32)
        gain_bc = alloc("gain_bc", [128, H], F32)
        eps_col = alloc("eps_col", [128, 1], F32)
        nhalf_col = alloc("nhalf_col", [128, 1], F32)
        gs_col = alloc("gs_col", [128, NIB], F32)
        us_col = alloc("us_col", [128, NIB], F32)
        st_mean = alloc("st_mean", [128, 16], F32)
        st_s2 = alloc("st_s2", [128, 16], F32)
        st_t1 = alloc("st_t1", [128, 16], F32)
        st_var = alloc("st_var", [128, 16], F32)
        st_rstd = alloc("st_rstd", [128, 16], F32)
        st_s2b = alloc("st_s2b", [128, 16], F32)
        st_mnb = alloc("st_mnb", [128, 16], F32)
        sk_a = alloc("sk_a", [128, 4], F32)
        sk_b = alloc("sk_b", [128, 4], F32)
        sq_a = alloc("sq_a", [128, 16], F32)
        sq_b = alloc("sq_b", [128, 16], F32)
        nmx_c = alloc("nmx_c", [128, 4], F32)
        z_c = alloc("z_c", [128, 4], F32)
        rz_c = alloc("rz_c", [128, 4], F32)
        mx_c = alloc("mx_c", [128, 4], F32)

        bigA = nc.alloc_sbuf_tensor("bigA", [128, 16, 2048], BF16)
        xnT = bigA.ap()
        ring_h2 = bigA.ap()
        wq_sb = bigA.ap()

        bigB = nc.alloc_sbuf_tensor("bigB", [128, 16, 1024], F32)
        B_addr = nc.lookup_mloc(bigB).addr
        xacc = bigB.ap()
        kT_sb = nc.alloc_sbuf_tensor_at("kT_sb", [128, KVH, S], BF16, offset=B_addr).ap()
        v_sb = nc.alloc_sbuf_tensor_at("v_sb", [128, NSB, 512], BF16, offset=B_addr + 16 * KB).ap()
        qT_sb = nc.alloc_sbuf_tensor_at("qT_sb", [128, H, 512], BF16, offset=B_addr + 32 * KB).ap()
        kv_wv = nc.alloc_sbuf_tensor_at("kv_wv", [128, NHB, 512], BF16, offset=B_addr + 32 * KB).ap()
        yT_sb = nc.alloc_sbuf_tensor_at("yT_sb", [128, H, 512], BF16, offset=B_addr + 48 * KB).ap()
        kstage = nc.alloc_sbuf_tensor_at("kstage", [128, 1024], F32, offset=B_addr + 48 * KB).ap()
        kvrow = nc.alloc_sbuf_tensor_at("kvrow", [128, 512], F32, offset=B_addr + 52 * KB).ap()
        qrot2 = nc.alloc_sbuf_tensor_at("qrot2", [128, D], F32, offset=B_addr + 48 * KB).ap()

        bigC = nc.alloc_sbuf_tensor("bigC", [128, 45 * KB], mybir.dt.uint8)
        C = nc.lookup_mloc(bigC).addr
        at_ = nc.alloc_sbuf_tensor_at
        wgu = at_("wgu", [128, 2, 2, NHB, 128], BF16, offset=C).ap()
        h2t = at_("h2t", [128, 2, D], BF16, offset=C + 16 * KB).ap()
        silu_b = at_("silu_b", [128, 2, 512], F32, offset=C + 24 * KB).ap()
        ring_dw = at_("ring_dw", [128, 16, 1024], BF16, offset=C).ap()
        ds_bc = at_("ds_bc", [128, D], F32, offset=C + 32 * KB).ap()
        ctile = at_("ctile", [128, H, 64], F32, offset=C + 21 * KB).ap()
        stile = at_("stile", [128, H, 64], F32, offset=C + 25 * KB).ap()
        xn_f = at_("xn_f", [128, D], F32, offset=C + 29 * KB).ap()
        xnT1c = at_("xnT1c", [128, 2, NHB, 128], BF16, offset=C + 37 * KB).ap()
        kv_wk = at_("kv_wk", [128, NHB, 512], BF16, offset=C).ap()


        krot = at_("krot", [128, 2, 512], F32, offset=C + 16 * KB).ap()
        qrot = at_("qrot", [128, D], F32, offset=C + 8 * KB).ap()
        qcross = at_("qcross", [128, H, 128], F32, offset=C + 32 * KB).ap()
        wo_row = at_("wo_row", [128, 2, D], BF16, offset=C).ap()
        stdump = at_("stdump", [128, 1024], F32, offset=C + 40 * KB).ap()
        probs_f = at_("probs_f", [128, 3, 640], F32, offset=C + 8 * KB).ap()
        attnT_b = at_("attnT_b", [128, 2, 5, 128], BF16, offset=C + 15 * KB + 512).ap()
        mask_sb = at_("mask_sb", [128, 640], F32, offset=C + 18 * KB).ap()

        psA = nc.alloc_psum_tensor("psA", [128, 2048], F32).ap()
        psB = nc.alloc_psum_tensor("psB", [128, 2048], F32).ap()

        sem_handles = {k: es.enter_context(nc.semaphore(k))
                       for k in ("sy", "sq", "gp", "pe", "ve", "ac")}

        Sc = Script()
        M = {}

        Sc.dma("sync", ident[:], c_ident, "sy")
        Sc.dma("sync", gs_col[:], g_sc, "sy")
        Sc.dma("sync", us_col[:], u_sc, "sy")
        Sc.dma("sync", gainrow[:], q_gain, "sy")
        Sc.dma("sync", mask_sb[:], c_mask, "sy")
        M["const_sy"] = Sc.current("sy")
        Sc.wait("vector", "sy", M["const_sy"])
        Sc.op("vector", lambda v: v.memset(ones1[:], 1.0), inc="ve")
        Sc.op("vector", lambda v: v.memset(eps_col[:], EPS_LN), inc="ve")
        Sc.op("vector", lambda v: v.memset(nhalf_col[:], -0.5), inc="ve")
        Sc.barrier()

        lnst = {"norm": {}, "sq": {}, "xnf_free": {}, "rstd": {}}
        tp_evac = {}

        def emit_ln_stats(xsrc, tag, buf=None, dump=None, gate=None):
            buf = xt if buf is None else buf
            dump = xn_f if dump is None else dump
            for sb in range(NSB):
                par = sb % 2
                Sc.wait("sync", "ac", lnst["sq"].get((tag, "st", sb - 2)))
                Sc.wait("sync", "ve", lnst["sq"].get((tag, "sv", sb - 2)))
                if gate is not None:
                    Sc.wait("sync", "sy", gate(sb))
                ld = Sc.dma("sync", buf[:, par, :], xsrc[sb * 128:(sb + 1) * 128, :], "sy")
                Sc.wait("scalar", "sy", ld)
                lnst["sq"][(tag, "st", sb)] = Sc.op(
                    "scalar", lambda s, par=par, sb=sb, buf=buf, dump=dump: s.activation(
                        dump[:], buf[:, par, :], AF.Square,
                        accum_out=st_s2[:, sb:sb + 1]), inc="ac")
                Sc.wait("vector", "sy", ld)
                lnst["sq"][(tag, "sv", sb)] = Sc.op(
                    "vector", lambda v, par=par, sb=sb, buf=buf: v.reduce_sum(
                        st_mean[:, sb:sb + 1], buf[:, par, :], axis=AX.X), inc="ve")
            Sc.fence("vector", "ve")
            Sc.op("vector", lambda v: v.tensor_scalar(
                st_mean[:, :], st_mean[:, :], 1.0 / D, None, ALU.mult), inc="ve")
            Sc.fence("vector", "ve")
            Sc.op("vector", lambda v: v.tensor_tensor(
                st_t1[:, :], st_mean[:, :], st_mean[:, :], ALU.mult), inc="ve")
            Sc.fence("vector", "ve")
            Sc.wait("vector", "ac", lnst["sq"][(tag, "st", NSB - 1)])
            vd = Sc.op("vector", lambda v: v.scalar_tensor_tensor(
                st_var[:, :], st_s2[:, :], 1.0 / D, st_t1[:, :],
                ALU.mult, ALU.subtract), inc="ve")
            Sc.wait("scalar", "ve", vd)
            Sc.op("scalar", lambda s: s.activation(
                st_t1[:, :], st_var[:, :], AF.Ln, bias=eps_col[:]), inc="ac")
            rstd = Sc.op("scalar", lambda s: s.activation(
                st_rstd[:, :], st_t1[:, :], AF.Exp, scale=nhalf_col[:]), inc="ac")
            lnst["rstd"][tag] = rstd

        def emit_ln_tile(xsrc, sb, tag, stats_tag=None):
            par = sb % 2
            key = (tag, sb - 2)
            Sc.wait("sync", "ve", lnst["norm"].get(key))
            if sb < 2:
                Sc.wait("sync", "ac", lnst["rstd"][stats_tag or tag])
            ld = Sc.dma("sync", xt[:, par, :], xsrc[sb * 128:(sb + 1) * 128, :], "sy")
            c = sb % 16
            Sc.wait("vector", "sy", ld)
            Sc.wait("vector", "ac", lnst["rstd"][stats_tag or tag])
            xf = lnst["xnf_free"].get((tag, sb - 1))
            if xf is not None:
                Sc.wait("vector", xf[0], xf[1])
            norm = Sc.op("vector", lambda v, par=par, c=c: v.tensor_scalar(
                xn_f[:], xt[:, par, :], st_mean[:, c:c + 1], st_rstd[:, c:c + 1],
                ALU.subtract, ALU.mult), inc="ve")
            lnst["norm"][(tag, sb)] = norm
            return par, norm

        def emit_ln_tile_chained(xsrc, sb, tag):
            par = sb % 2
            key = (tag, sb - 2)
            Sc.wait("sync", "ve", lnst["norm"].get(key))
            Sc.wait("sync", "ac", lnst["sq"].get(key))
            ld = Sc.dma("sync", xt[:, par, :], xsrc[sb * 128:(sb + 1) * 128, :], "sy")
            c = sb % 16
            Sc.wait("scalar", "sy", ld)
            xf = lnst["xnf_free"].get((tag, sb - 1))
            if xf is not None:
                Sc.wait("scalar", xf[0], xf[1])
            sq = Sc.op("scalar", lambda s, par=par, c=c: s.activation(
                xn_f[:], xt[:, par, :], AF.Square, accum_out=st_s2[:, c:c + 1]), inc="ac")
            lnst["sq"][(tag, sb)] = sq
            Sc.wait("vector", "sy", ld)
            Sc.op("vector", lambda v, par=par, c=c: v.reduce_sum(
                st_mean[:, c:c + 1], xt[:, par, :], axis=AX.X), inc="ve")
            Sc.fence("vector", "ve")
            mean_done = Sc.op("vector", lambda v, c=c: v.tensor_scalar(
                st_mean[:, c:c + 1], st_mean[:, c:c + 1], 1.0 / D, None, ALU.mult), inc="ve")
            Sc.fence("vector", "ve")
            Sc.op("vector", lambda v, c=c: v.tensor_tensor(
                st_t1[:, c:c + 1], st_mean[:, c:c + 1], st_mean[:, c:c + 1], ALU.mult), inc="ve")
            Sc.fence("vector", "ve")
            Sc.wait("vector", "ac", sq)
            var_done = Sc.op("vector", lambda v, c=c: v.scalar_tensor_tensor(
                st_var[:, c:c + 1], st_s2[:, c:c + 1], 1.0 / D, st_t1[:, c:c + 1],
                ALU.mult, ALU.subtract), inc="ve")
            Sc.wait("scalar", "ve", var_done)
            Sc.op("scalar", lambda s, c=c: s.activation(
                st_t1[:, c:c + 1], st_var[:, c:c + 1], AF.Ln, bias=eps_col[:]), inc="ac")
            rstd = Sc.op("scalar", lambda s, c=c: s.activation(
                st_rstd[:, c:c + 1], st_t1[:, c:c + 1], AF.Exp, scale=nhalf_col[:]), inc="ac")
            Sc.wait("vector", "ac", rstd)
            Sc.wait("vector", "ve", mean_done)
            norm = Sc.op("vector", lambda v, par=par, c=c: v.tensor_scalar(
                xn_f[:], xt[:, par, :], st_mean[:, c:c + 1], st_rstd[:, c:c + 1],
                ALU.subtract, ALU.mult), inc="ve")
            lnst["norm"][(tag, sb)] = norm
            return par, norm

        def emit_transposes(src_ap, n_tiles, dest_fn, src_off=0, evac_extra_war=None):
            last_pe = last_ve = None
            j0 = 0
            g = 0
            while j0 < n_tiles:
                cnt = min(8, n_tiles - j0)
                rot = emit_transposes.rot
                emit_transposes.rot ^= 1
                Sc.wait("tensor", "ve", tp_evac.get(rot))
                for j in range(cnt):
                    pev = Sc.op("tensor", lambda t, rot=rot, j=j, jj=j0 + j: t.transpose(
                        psB[:, rot * 1024 + j * 128: rot * 1024 + (j + 1) * 128],
                        src_ap[:, src_off + jj * 128: src_off + (jj + 1) * 128], ident[:]),
                        inc="pe" if j == cnt - 1 else None)
                Sc.wait("vector", "pe", pev)
                if evac_extra_war is not None:
                    Sc.wait("vector", evac_extra_war[0], evac_extra_war[1])
                ev = Sc.op("vector", lambda v, rot=rot, g=g, cnt=cnt: v.tensor_copy(
                    dest_fn(g, cnt),
                    psB[:, rot * 1024: rot * 1024 + cnt * 128].rearrange(
                        "p (j f) -> p j f", j=cnt)), inc="ve")
                tp_evac[rot] = ev
                last_pe, last_ve = pev, ev
                j0 += cnt
                g += 1
            return last_pe, last_ve
        emit_transposes.rot = 0

        def emit_ffn(call, xsrc, xdst, a1_stats=False):
            tag = f"f{call}"
            for sb in range(NSB):
                emit_ln_tile_chained(xsrc, sb, tag)
                Sc.wait("tensor", "ve", lnst["norm"][(tag, sb)])
                pev, _ = emit_transposes(
                    xn_f, 16,
                    lambda g, cnt, sb=sb: xnT[:, g * 8:g * 8 + cnt, sb * 128:(sb + 1) * 128])
                lnst["xnf_free"][(tag, sb)] = ("pe", pev)
            Sc.barrier()

            mmdone = {}
            h2qd = {}
            h2store = {}
            for ib in range(NIB):
                par = ib % 2
                war_pe = mmdone.get(ib - 2)
                if call == 1:
                    Sc.wait("gpsimd", "pe", war_pe)
                    Sc.wait("gpsimd", "sq", M.get((tag, "wst", ib - 2)))
                    Sc.dma("gpsimd", wgu[:, par, 0, :, :],
                           gw_t[:, ib * 128:(ib + 1) * 128].rearrange("(hb p) i -> p hb i", p=128), "gp")
                    g2 = Sc.dma("gpsimd", wgu[:, par, 1, :, :],
                                uw_t[:, ib * 128:(ib + 1) * 128].rearrange("(hb p) i -> p hb i", p=128), "gp")
                    Sc.wait("vector", "gp", g2)
                    Sc.op("vector", lambda v, par=par: v.tensor_scalar(
                        wgu[:, par, 0].rearrange("p a b -> p (a b)"),
                        wgu[:, par, 0].rearrange("p a b -> p (a b)"),
                        -128.0, None, ALU.add), inc="ve")
                    wv = Sc.op("vector", lambda v, par=par: v.tensor_scalar(
                        wgu[:, par, 1].rearrange("p a b -> p (a b)"),
                        wgu[:, par, 1].rearrange("p a b -> p (a b)"),
                        -128.0, None, ALU.add), inc="ve")
                    wrdy = ("ve", wv)
                    Sc.wait("scalar", "ve", wv)
                    Sc.dma("scalar", gw_sc[ib], wgu[:, par, 0].rearrange("p a b -> p (a b)"), "sq")
                    M[(tag, "wst", ib)] = Sc.dma(
                        "scalar", uw_sc[ib], wgu[:, par, 1].rearrange("p a b -> p (a b)"), "sq")
                else:
                    Sc.wait("scalar", "pe", war_pe)
                    Sc.dma("scalar", wgu[:, par, 0].rearrange("p a b -> p (a b)"), gw_sc[ib], "sq")
                    wv = Sc.dma("scalar", wgu[:, par, 1].rearrange("p a b -> p (a b)"), uw_sc[ib], "sq")
                    wrdy = ("sq", wv)
                for q4 in range(4):
                    gq = ib * 4 + q4
                    qpar = gq % 2
                    if q4 == 0:
                        Sc.wait("tensor", wrdy[0], wrdy[1])
                    Sc.wait("tensor", "ve", h2qd.get(gq - 2))
                    for hb in range(NHB):
                        gv = Sc.op("tensor", lambda t, par=par, qpar=qpar, hb=hb, q4=q4: t.matmul(
                            psA[:, qpar * 512:(qpar + 1) * 512],
                            wgu[:, par, 0, hb, :],
                            xnT[:, hb, q4 * 512:(q4 + 1) * 512],
                            start=(hb == 0), stop=(hb == NHB - 1)),
                            inc="pe" if hb == NHB - 1 else None)
                    for hb in range(NHB):
                        uv = Sc.op("tensor", lambda t, par=par, qpar=qpar, hb=hb, q4=q4: t.matmul(
                            psA[:, 1024 + qpar * 512:1024 + (qpar + 1) * 512],
                            wgu[:, par, 1, hb, :],
                            xnT[:, hb, q4 * 512:(q4 + 1) * 512],
                            start=(hb == 0), stop=(hb == NHB - 1)),
                            inc="pe" if hb == NHB - 1 else None)
                    Sc.wait("scalar", "pe", gv)
                    Sc.wait("scalar", "ve", h2qd.get(gq - 2))
                    sil = Sc.op("scalar", lambda s, qpar=qpar, ib=ib: s.activation(
                        silu_b[:, qpar, :], psA[:, qpar * 512:(qpar + 1) * 512], AF.Silu,
                        scale=gs_col[:, ib:ib + 1]), inc="ac")
                    Sc.wait("vector", "pe", uv)
                    Sc.wait("vector", "ac", sil)
                    if q4 == 0:
                        Sc.wait("vector", "sy", h2store.get(ib - 2))
                    h2qd[gq] = Sc.op("vector", lambda v, par=par, qpar=qpar, ib=ib, q4=q4: v.scalar_tensor_tensor(
                        h2t[:, par, q4 * 512:(q4 + 1) * 512],
                        psA[:, 1024 + qpar * 512:1024 + (qpar + 1) * 512],
                        us_col[:, ib:ib + 1],
                        silu_b[:, qpar, :], ALU.mult, ALU.mult), inc="ve")
                mmdone[ib] = Sc.current("pe")
                Sc.wait("sync", "ve", h2qd[ib * 4 + 3])
                h2store[ib] = Sc.dma("sync", h2_sc[ib], h2t[:, par, :], "sy")
            Sc.barrier()

            if call == 1:
                dsr = Sc.dma("sync", ds_bc[0:1, :], d_sc, "sy")
                Sc.wait("tensor", "sy", dsr)
                for j in range(4):
                    bcv = Sc.op("tensor", lambda t, j=j: t.matmul(
                        psB[:, j * 512:(j + 1) * 512], ones1[:], ds_bc[0:1, j * 512:(j + 1) * 512],
                        start=True, stop=True), inc="pe" if j == 3 else None)
                Sc.wait("vector", "pe", bcv)
                dsv = Sc.op("vector", lambda v: v.tensor_scalar(
                    ds_bc[:], psB[:, :], 0.5, None, ALU.mult), inc="ve")
                Sc.wait("tensor", "ve", dsv)

            psum_war = {}
            chunk_mm = {}
            resd_last = 0
            NCH = 8
            KCH = 8
            for hh in range(2):
                for ch0 in range(NCH):
                    ch = hh * NCH + ch0
                    base = (ch % 2) * (16 - KCH)
                    rdy_sy = rdy_ve = rdy_sq = 0
                    for k8 in range(KCH):
                        k = base + k8
                        ib = ch0 * KCH + k8
                        gib = ch * KCH + k8
                        war = chunk_mm.get(ch - 2)
                        Sc.wait("sync", "pe", war)
                        rdy_sy = Sc.dma("sync", ring_h2[:, k, :], h2_sc[ib], "sy")
                        if call == 1:
                            Sc.wait("gpsimd", "pe", war)
                            Sc.wait("gpsimd", "sq", M.get((tag, "dwst", gib - 2 * KCH)))
                            cd = Sc.dma("gpsimd", ring_dw[:, k, :],
                                        dw_t[ib * 128:(ib + 1) * 128, hh * 1024:(hh + 1) * 1024], "gp")
                            Sc.wait("vector", "gp", cd)
                            Sc.op("vector", lambda v, k=k: v.tensor_scalar(
                                ring_dw[:, k, :], ring_dw[:, k, :], -128.0, None, ALU.add), inc="ve")
                            rdy_ve = Sc.op("vector", lambda v, k=k, hh=hh: v.tensor_tensor(
                                ring_dw[:, k, :], ring_dw[:, k, :],
                                ds_bc[:, hh * 1024:(hh + 1) * 1024], ALU.mult), inc="ve")
                            Sc.wait("scalar", "ve", rdy_ve)
                            M[(tag, "dwst", gib)] = Sc.dma(
                                "scalar", dw_sc[ib][:, hh * 1024:(hh + 1) * 1024], ring_dw[:, k, :], "sq")

                        else:
                            Sc.wait("scalar", "pe", war)
                            rdy_sq = Sc.dma("scalar", ring_dw[:, k, :],
                                            dw_sc[ib][:, hh * 1024:(hh + 1) * 1024], "sq")
                    dwkey = "ve" if call == 1 else "sq"
                    Sc.wait("tensor", "sy", rdy_sy)
                    Sc.wait("tensor", dwkey, rdy_ve if call == 1 else rdy_sq)
                    for sb in range(NSB):
                        spar = sb % 2
                        ps = psA if spar == 0 else psB
                        Sc.wait("tensor", "ve", psum_war.get(spar))
                        for k8 in range(KCH):
                            k = base + k8
                            for n2 in range(2):
                                pv = Sc.op("tensor", lambda t, ps=ps, k=k, n2=n2, sb=sb, k8=k8: t.matmul(
                                    ps[:, n2 * 512:(n2 + 1) * 512],
                                    ring_h2[:, k, sb * 128:(sb + 1) * 128],
                                    ring_dw[:, k, n2 * 512:(n2 + 1) * 512],
                                    start=(k8 == 0), stop=(k8 == KCH - 1)),
                                    inc="pe" if (k8 == KCH - 1 and n2 == 1) else None)
                        Sc.wait("vector", "pe", pv)
                        if ch0 == 0:
                            if hh == 1:
                                Sc.wait("vector", "sy", M.get((tag, 0, "hst"), 0))
                                if a1_stats:
                                    Sc.wait("vector", "ac", M[(tag, 0, "sq", sb)])
                            av = Sc.op("vector", lambda v, ps=ps, sb=sb: v.tensor_copy(
                                xacc[:, sb, :], ps[:, 0:1024]), inc="ve")
                        else:
                            av = Sc.op("vector", lambda v, ps=ps, sb=sb: v.tensor_tensor(
                                xacc[:, sb, :], xacc[:, sb, :], ps[:, 0:1024], ALU.add), inc="ve")
                        psum_war[spar] = av
                    chunk_mm[ch] = Sc.current("pe")
                resd = {}
                stv = 0
                for sb in range(NSB):
                    par = sb % 2
                    Sc.wait("sync", "ve", resd.get(sb - 2))
                    if sb < 2:
                        Sc.wait("sync", "ve", resd_last)
                    ld = Sc.dma("sync", xt[:, par, 0:1024],
                                xsrc[sb * 128:(sb + 1) * 128, hh * 1024:(hh + 1) * 1024], "sy")
                    Sc.wait("vector", "sy", ld)
                    rv = Sc.op("vector", lambda v, sb=sb, par=par: v.tensor_tensor(
                        xacc[:, sb, :], xacc[:, sb, :], xt[:, par, 0:1024], ALU.add), inc="ve")
                    resd[sb] = rv
                    if a1_stats:
                        s2d = st_s2 if hh == 0 else st_s2b
                        mnd = st_mean if hh == 0 else st_mnb
                        Sc.wait("scalar", "ve", rv)
                        Sc.op("scalar", lambda s, sb=sb, s2d=s2d: s.activation(
                            stdump[:], xacc[:, sb, :], AF.Square,
                            accum_out=s2d[:, sb:sb + 1]), inc="ac")
                        M[(tag, hh, "sq", sb)] = Sc.op(
                            "scalar", lambda s, sb=sb, mnd=mnd: s.activation(
                                stdump[:], xacc[:, sb, :], AF.Identity,
                                accum_out=mnd[:, sb:sb + 1]), inc="ac")
                    Sc.wait("sync", "ve", rv)
                    stv = Sc.dma("sync", xdst[sb * 128:(sb + 1) * 128, hh * 1024:(hh + 1) * 1024],
                                 xacc[:, sb, :], "sy")
                    M[(tag, hh, "sb_st", sb)] = stv
                M[(tag, hh, "hst")] = stv
                resd_last = resd[NSB - 1]
            if True:
                if a1_stats:
                    Sc.wait("gpsimd", "pe", chunk_mm[2 * NCH - 2])
                    Sc.wait("gpsimd", "sq", M[(tag, "dwst", 14 * KCH + KCH - 1)])
                    M["kvw1"] = Sc.dma("gpsimd", kv_wk[:, :, :],
                                       wk_t.rearrange("(hb p) e -> p hb e", p=128), "gp")
                    Sc.wait("gpsimd", "sy", M[(tag, 1, "sb_st", 11)])
                    Sc.wait("gpsimd", "ac", M[(tag, 1, "sq", 11)])
                    M["kvw2"] = Sc.dma("gpsimd", kv_wv[:, :, :],
                                       wv_t.rearrange("(hb p) e -> p hb e", p=128), "gp")
                    Sc.wait("gpsimd", "pe", chunk_mm[2 * NCH - 1])
                    M["wqv"] = Sc.dma("gpsimd", wq_sb[:, :, :],
                                      wq_t.rearrange("(hb p) e -> p hb e", p=128), "gp")
                    M["wost"] = Sc.dma(
                        "gpsimd", wo_sc[:, :, :],
                        wo_t.rearrange("(db p) e -> db p e", p=128), "gp")
                    Sc.wait("vector", "ac", M[(tag, 1, "sq", NSB - 1)])
                    Sc.op("vector", lambda v: v.tensor_tensor(
                        st_s2[:, :], st_s2[:, :], st_s2b[:, :], ALU.add), inc="ve")
                    Sc.op("vector", lambda v: v.tensor_tensor(
                        st_mean[:, :], st_mean[:, :], st_mnb[:, :], ALU.add), inc="ve")
                    Sc.fence("vector", "ve")
                    Sc.op("vector", lambda v: v.tensor_scalar(
                        st_mean[:, :], st_mean[:, :], 1.0 / D, None, ALU.mult), inc="ve")
                    Sc.fence("vector", "ve")
                    Sc.op("vector", lambda v: v.tensor_tensor(
                        st_t1[:, :], st_mean[:, :], st_mean[:, :], ALU.mult), inc="ve")
                    Sc.fence("vector", "ve")
                    vd = Sc.op("vector", lambda v: v.scalar_tensor_tensor(
                        st_var[:, :], st_s2[:, :], 1.0 / D, st_t1[:, :],
                        ALU.mult, ALU.subtract), inc="ve")
                    Sc.wait("scalar", "ve", vd)
                    Sc.op("scalar", lambda s: s.activation(
                        st_t1[:, :], st_var[:, :], AF.Ln, bias=eps_col[:]), inc="ac")
                    rstd = Sc.op("scalar", lambda s: s.activation(
                        st_rstd[:, :], st_t1[:, :], AF.Exp, scale=nhalf_col[:]), inc="ac")
                    lnst["rstd"]["a1"] = rstd
                if a1_stats:
                    Sc.barrier(engines=("tensor", "vector", "scalar", "sync"))
                else:
                    Sc.barrier()

        def emit_attn(attn_dst, dbg=None):
            kvw2 = M["kvw2"]
            gb = Sc.op("tensor", lambda t: t.matmul(psB[:, 0:H], ones1[:], gainrow[:],
                                                    start=True, stop=True), inc="pe")
            Sc.wait("vector", "pe", gb)
            Sc.op("vector", lambda v: v.tensor_scalar(
                gain_bc[:], psB[:, 0:H], INVSQRT_HD, None, ALU.mult), inc="ve")

            kvmm = {}
            kevd = {}

            def emit_kv_front(sb):
                par = sb % 2
                emit_ln_tile(x1_sc, sb, "akv", stats_tag="a1")
                Sc.wait("sync", "ve", M.get(("krd", sb - 1)))
                rc = Sc.dma("sync", ctile[:, 0:KVH, :],
                            c_cos[sb * 128:(sb + 1) * 128, 0:KVH, :], "sy")
                rs = Sc.dma("sync", stile[:, 0:KVH, :],
                            c_sin[sb * 128:(sb + 1) * 128, 0:KVH, :], "sy")
                Sc.wait("tensor", "ve", lnst["norm"][("akv", sb)])
                if sb == 0:
                    Sc.wait("vector", "gp", M["kvw2"])
                pev, _ = emit_transposes(
                    xn_f, 16,
                    lambda g, cnt, par=par: xnT1c[:, par, g * 8:g * 8 + cnt, :],
                    evac_extra_war=("pe", kvmm.get(sb - 2)))
                lnst["xnf_free"][("akv", sb)] = ("pe", pev)
                if sb == 0:
                    Sc.wait("tensor", "gp", M["kvw2"])
                kslot = par * 512
                vslot = 1024 + par * 512
                Sc.wait("tensor", "ac", kevd.get(sb - 2))
                for hb in range(NHB):
                    kv_mm = Sc.op("tensor", lambda t, par=par, hb=hb, kslot=kslot: t.matmul(
                        psA[:, kslot:kslot + 512], xnT1c[:, par, hb, :], kv_wk[:, hb, :],
                        start=(hb == 0), stop=(hb == NHB - 1)),
                        inc="pe" if hb == NHB - 1 else None)
                for hb in range(NHB):
                    vv_mm = Sc.op("tensor", lambda t, par=par, hb=hb, vslot=vslot: t.matmul(
                        psA[:, vslot:vslot + 512], xnT1c[:, par, hb, :], kv_wv[:, hb, :],
                        start=(hb == 0), stop=(hb == NHB - 1)),
                        inc="pe" if hb == NHB - 1 else None)
                kvmm[sb] = Sc.current("pe")
                Sc.wait("scalar", "pe", vv_mm)
                Sc.op("scalar", lambda s, sb=sb, vslot=vslot: s.activation(
                    v_sb[:, sb, :], psA[:, vslot:vslot + 512], AF.Identity), inc="ac")
                Sc.wait("scalar", "ve", M.get(("krd", sb - 1)))
                kev = Sc.op("scalar", lambda s, kslot=kslot: s.activation(
                    kvrow[:], psA[:, kslot:kslot + 512], AF.Identity), inc="ac")
                kevd[sb] = kev
                Sc.wait("vector", "ac", kev)
                Sc.op("vector", lambda v: v.tensor_tensor(
                    kstage[:, 0:512], kvrow[:], kvrow[:], ALU.mult), inc="ve")
                Sc.fence("vector", "ve")
                Sc.op("vector", lambda v: v.reduce_sum(
                    sk_a[:, :], kstage[:, 0:512].rearrange("p (g f) -> p g f", g=KVH),
                    axis=AX.X), inc="ve")
                Sc.fence("vector", "ve")
                rms1 = Sc.op("vector", lambda v: v.tensor_scalar(
                    sk_a[:, :], sk_a[:, :], 1.0 / HD, EPS_RMS, ALU.mult, ALU.add), inc="ve")
                Sc.wait("scalar", "ve", rms1)
                Sc.op("scalar", lambda s: s.activation(sk_b[:, :], sk_a[:, :], AF.Ln), inc="ac")
                krst = Sc.op("scalar", lambda s: s.activation(
                    sk_a[:, :], sk_b[:, :], AF.Exp, scale=nhalf_col[:]), inc="ac")
                Sc.wait("vector", "ac", krst)
                for g in range(KVH):
                    Sc.op("vector", lambda v, g=g: v.tensor_scalar(
                        kvrow[:, g * 128:(g + 1) * 128], kvrow[:, g * 128:(g + 1) * 128],
                        sk_a[:, g:g + 1], None, ALU.mult), inc="ve")
                Sc.fence("vector", "ve")
                Sc.wait("vector", "sy", rs)
                Sc.wait("vector", "pe", M.get(("ktp", sb - 2)))
                k3 = kvrow.rearrange("p (g f) -> p g f", g=KVH)
                r3 = krot[:, sb % 2, :].rearrange("p (g f) -> p g f", g=KVH)
                u3 = kstage[:, 512:1024].rearrange("p (g f) -> p g f", g=KVH)
                ct4 = ctile[:, 0:KVH, :]
                st4 = stile[:, 0:KVH, :]
                Sc.op("vector", lambda v, r3=r3: v.tensor_tensor(r3[:, :, 0:64], k3[:, :, 0:64], ct4, ALU.mult), inc="ve")
                Sc.op("vector", lambda v: v.tensor_tensor(u3[:, :, 0:64], k3[:, :, 64:128], st4, ALU.mult), inc="ve")
                Sc.fence("vector", "ve")
                Sc.op("vector", lambda v, r3=r3: v.tensor_tensor(r3[:, :, 0:64], r3[:, :, 0:64], u3[:, :, 0:64], ALU.add), inc="ve")
                Sc.op("vector", lambda v, r3=r3: v.tensor_tensor(r3[:, :, 64:128], k3[:, :, 64:128], ct4, ALU.mult), inc="ve")
                Sc.op("vector", lambda v: v.tensor_tensor(u3[:, :, 64:128], k3[:, :, 0:64], st4, ALU.mult), inc="ve")
                Sc.fence("vector", "ve")
                krd = Sc.op("vector", lambda v, r3=r3: v.tensor_tensor(r3[:, :, 64:128], r3[:, :, 64:128], u3[:, :, 64:128], ALU.subtract), inc="ve")
                M[("krd", sb)] = krd

            def emit_kT_tp(sb):
                Sc.wait("tensor", "ve", M[("krd", sb)])
                ktp, kte = emit_transposes(
                    krot[:, sb % 2, :], 4,
                    lambda g, cnt, sb=sb: kT_sb[:, g * 8:g * 8 + cnt, sb * 128:(sb + 1) * 128])
                M[("ktp", sb)] = ktp

            for sb in range(NSB):
                emit_kv_front(sb)
                if sb > 0:
                    emit_kT_tp(sb - 1)
            emit_kT_tp(NSB - 1)
            Sc.barrier()
            if dbg == "dk":
                Sc.dma("gpsimd", out[0:512, :].rearrange("(g p) s -> p g s", p=128), kT_sb[:, :, :], "gp")
                return
            if dbg == "dv":
                Sc.dma("gpsimd", out.rearrange("(sb p) e -> p sb e", p=128)[:, :, 0:512], v_sb[:, :, :], "gp")
                return

            wqv = M["wqv"]
            Sc.dma("sync", mask_sb[:], c_mask, "sy")
            Sc.wait("vector", "gp", M["wost"])
            Sc.barrier()
            for chunk in range(4):
                def emit_q_front(sbi, chunk=chunk):
                    sb = chunk * 4 + sbi
                    par = sb % 2
                    if sbi == 0:
                        Sc.wait("sync", "ve", Sc.current("ve"))
                        Sc.wait("vector", "pe", M.get(("ptp", chunk - 1, 63)))
                        Sc.wait("vector", "sy", M.get(("rst", chunk - 1, 1, 1)))
                    if sbi == 1 and chunk > 0:
                        Sc.wait("vector", "pe", M.get(("opmm", chunk - 1, 1, NHB - 1)))
                    emit_ln_tile(x1_sc, sb, "aq", stats_tag="a1")
                    Sc.wait("sync", "ve", M.get(("qrd", sb - 1)))
                    rc = Sc.dma("sync", ctile[:, :, :],
                                c_cos[sb * 128:(sb + 1) * 128, :, :], "sy")
                    rs = Sc.dma("sync", stile[:, :, :],
                                c_sin[sb * 128:(sb + 1) * 128, :, :], "sy")
                    Sc.wait("tensor", "ve", lnst["norm"][("aq", sb)])
                    emit_transposes(
                        xn_f, 16,
                        lambda g, cnt, par=par: xnT1c[:, par, g * 8:g * 8 + cnt, :])
                    if chunk == 0 and sbi == 0:
                        Sc.wait("tensor", "gp", wqv)
                    Sc.wait("tensor", "ac", M.get(("qev", sb - 2)))
                    for hb in range(NHB):
                        for n4 in range(4):
                            qmm = Sc.op("tensor", lambda t, par=par, hb=hb, n4=n4: t.matmul(
                                psA[:, n4 * 512:(n4 + 1) * 512], xnT1c[:, par, hb, :],
                                wq_sb[:, hb, n4 * 512:(n4 + 1) * 512],
                                start=(hb == 0), stop=(hb == NHB - 1)),
                                inc="pe" if (hb == NHB - 1 and n4 == 3) else None)
                    Sc.wait("scalar", "pe", qmm)
                    Sc.wait("scalar", "ve", M.get(("qap", sb - 1)))
                    qev = Sc.op("scalar", lambda s: s.activation(
                        xn_f[:], psA[:, :], AF.Identity), inc="ac")
                    M[("qev", sb)] = qev
                    Sc.wait("vector", "ac", qev)
                    Sc.op("vector", lambda v: v.tensor_tensor(
                        res_f[:, 0, :], xn_f[:], xn_f[:], ALU.mult), inc="ve")
                    Sc.fence("vector", "ve")
                    Sc.op("vector", lambda v: v.reduce_sum(
                        sq_a[:, :], res_f[:, 0, :].rearrange("p (g f) -> p g f", g=H),
                        axis=AX.X), inc="ve")
                    Sc.fence("vector", "ve")
                    qr1 = Sc.op("vector", lambda v: v.tensor_scalar(
                        sq_a[:, :], sq_a[:, :], 1.0 / HD, EPS_RMS, ALU.mult, ALU.add), inc="ve")
                    Sc.wait("scalar", "ve", qr1)
                    Sc.op("scalar", lambda s: s.activation(sq_b[:, :], sq_a[:, :], AF.Ln), inc="ac")
                    qrst = Sc.op("scalar", lambda s: s.activation(
                        sq_a[:, :], sq_b[:, :], AF.Exp, scale=nhalf_col[:]), inc="ac")
                    Sc.wait("vector", "ac", qrst)
                    fct = Sc.op("vector", lambda v: v.tensor_tensor(
                        sq_a[:, :], sq_a[:, :], gain_bc[:], ALU.mult), inc="ve")
                    Sc.wait("vector", "ve", fct)
                    for g in range(H):
                        Sc.op("vector", lambda v, g=g: v.tensor_scalar(
                            res_f[:, 1, g * 128:(g + 1) * 128], xn_f[:, g * 128:(g + 1) * 128],
                            sq_a[:, g:g + 1], None, ALU.mult), inc="ve")
                    qap = Sc.current("ve")
                    M[("qap", sb)] = qap
                    Sc.fence("vector", "ve")
                    Sc.wait("vector", "sy", rs)
                    Sc.wait("vector", "pe", M.get(("qtp", chunk * 4 + sbi - 2)))
                    rot = qrot if sbi % 2 == 0 else qrot2
                    q3 = res_f[:, 1, :].rearrange("p (g f) -> p g f", g=H)
                    o3 = rot.rearrange("p (g f) -> p g f", g=H)
                    w3 = qcross
                    Sc.op("vector", lambda v, q3=q3, o3=o3: v.tensor_tensor(
                        o3[:, :, 0:64], q3[:, :, 0:64], ctile[:], ALU.mult), inc="ve")
                    Sc.op("vector", lambda v, q3=q3, w3=w3: v.tensor_tensor(
                        w3[:, :, 0:64], q3[:, :, 64:128], stile[:], ALU.mult), inc="ve")
                    Sc.fence("vector", "ve")
                    Sc.op("vector", lambda v, o3=o3, w3=w3: v.tensor_tensor(
                        o3[:, :, 0:64], o3[:, :, 0:64], w3[:, :, 0:64], ALU.add), inc="ve")
                    Sc.op("vector", lambda v, q3=q3, o3=o3: v.tensor_tensor(
                        o3[:, :, 64:128], q3[:, :, 64:128], ctile[:], ALU.mult), inc="ve")
                    Sc.op("vector", lambda v, q3=q3, w3=w3: v.tensor_tensor(
                        w3[:, :, 64:128], q3[:, :, 0:64], stile[:], ALU.mult), inc="ve")
                    Sc.fence("vector", "ve")
                    qrd = Sc.op("vector", lambda v, o3=o3, w3=w3: v.tensor_tensor(
                        o3[:, :, 64:128], o3[:, :, 64:128], w3[:, :, 64:128], ALU.subtract), inc="ve")
                    M[("qrd", sb)] = qrd
                    lnst["xnf_free"][("aq", sb)] = ("ve", qap)

                def emit_qT_tp(sbi, chunk=chunk):
                    sb = chunk * 4 + sbi
                    rot = qrot if sbi % 2 == 0 else qrot2
                    Sc.wait("tensor", "ve", M[("qrd", sb)])
                    pev, _ = emit_transposes(
                        rot, 16,
                        lambda g, cnt, sbi=sbi: qT_sb[:, g * 8:g * 8 + cnt, sbi * 128:(sbi + 1) * 128])
                    M[("qtp", sb)] = pev

                for sbi in range(4):
                    emit_q_front(sbi)
                    if sbi > 0:
                        emit_qT_tp(sbi - 1)
                emit_qT_tp(3)
                Sc.barrier()
                if dbg == "dq":
                    Sc.dma("gpsimd", out[0:2048, chunk * 512:(chunk + 1) * 512].rearrange(
                        "(g p) s -> p g s", p=128), qT_sb[:, :, :], "gp")
                    if chunk == 3:
                        return
                    continue

                iters = []
                for qbi in range(4):
                    sb = chunk * 4 + qbi
                    s0 = sb * 128
                    t0 = max(0, s0 - 512)
                    w = s0 + 128 - t0
                    for kvh in range(KVH):
                        for qh in range(4):
                            iters.append((qbi, kvh, qh, t0, w))
                NG = len(iters)
                SLOTS = [psB[:, 0:640], psB[:, 1024:1664]]

                def emit_scores(g2):
                    qbi, kvh, qh, t0, w = iters[g2]
                    head = kvh * 4 + qh
                    sl = SLOTS[g2 % 2]
                    Sc.wait("tensor", "ac", M.get(("exp", chunk, g2 - 2)))
                    n1 = min(w, 512)
                    Sc.op("tensor", lambda t, sl=sl, head=head, qbi=qbi, kvh=kvh, t0=t0, n1=n1: t.matmul(
                        sl[:, 0:n1],
                        qT_sb[:, head, qbi * 128:(qbi + 1) * 128],
                        kT_sb[:, kvh, t0:t0 + n1],
                        start=True, stop=True),
                        inc="pe" if w <= 512 else None)
                    if w > 512:
                        Sc.op("tensor", lambda t, sl=sl, head=head, qbi=qbi, kvh=kvh, t0=t0, w=w: t.matmul(
                            sl[:, 512:w],
                            qT_sb[:, head, qbi * 128:(qbi + 1) * 128],
                            kT_sb[:, kvh, t0 + 512:t0 + w],
                            start=True, stop=True), inc="pe")
                    M[("scmm", chunk, g2)] = Sc.current("pe")

                def emit_stage1(g2):
                    qbi, kvh, qh, t0, w = iters[g2]
                    p3 = g2 % 3
                    sl = SLOTS[g2 % 2]
                    Sc.wait("vector", "pe", M[("scmm", chunk, g2)])
                    mkd = Sc.op("vector", lambda v, sl=sl, w=w: v.tensor_tensor(
                        sl[:, 0:w], sl[:, 0:w], mask_sb[:, 640 - w:640], ALU.add), inc="ve")
                    Sc.wait("scalar", "ve", mkd)
                    Sc.wait("scalar", "pe", M.get(("ptp", chunk, g2 - 3)))
                    ex = Sc.op("scalar", lambda s, sl=sl, w=w, p3=p3: s.activation(
                        probs_f[:, p3, 0:w], sl[:, 0:w], AF.Exp,
                        accum_out=z_c[:, p3:p3 + 1]), inc="ac")
                    M[("exp", chunk, g2)] = ex

                def emit_stage2(g2):
                    qbi, kvh, qh, t0, w = iters[g2]
                    p3 = g2 % 3
                    Sc.wait("vector", "ac", M[("exp", chunk, g2)])
                    rzv = Sc.op("vector", lambda v, p3=p3: v.reciprocal(
                        rz_c[:, p3:p3 + 1], z_c[:, p3:p3 + 1]), inc="ve")
                    Sc.wait("vector", "ve", rzv)
                    psv = Sc.op("vector", lambda v, p3=p3, w=w: v.tensor_scalar(
                        probs_f[:, p3, 0:w], probs_f[:, p3, 0:w],
                        rz_c[:, p3:p3 + 1], None, ALU.mult), inc="ve")
                    M[("psv", chunk, g2)] = psv

                def emit_ptav(g2, grp):
                    qbi, kvh, qh, t0, w = iters[g2]
                    nb = w // 128
                    p3 = g2 % 3
                    pp = g2 % 2
                    Sc.wait("tensor", "ve", M[("psv", chunk, g2)])
                    Sc.wait("tensor", "ac", M.get(("aev", chunk, g2 - 1)))
                    for j in range(nb):
                        tpv = Sc.op("tensor", lambda t, p3=p3, j=j: t.transpose(
                            psA[:, 512 + j * 128: 512 + (j + 1) * 128],
                            probs_f[:, p3, j * 128:(j + 1) * 128], ident[:]),
                            inc="pe" if j == nb - 1 else None)
                    M[("ptp", chunk, g2)] = tpv
                    Sc.wait("scalar", "pe", tpv)
                    Sc.wait("scalar", "pe", M.get(("avv", chunk, g2 - 2)))
                    Sc.op("scalar", lambda s, pp=pp, nb=nb: s.activation(
                        attnT_b[:, pp, 0:nb, :],
                        psA[:, 512: 512 + nb * 128].rearrange(
                            "p (j f) -> p j f", j=nb), AF.Identity), inc="ac")
                    M[("aev", chunk, g2)] = Sc.current("ac")
                    Sc.wait("tensor", "ac", M[("aev", chunk, g2)])
                    if qh == 0:
                        Sc.wait("tensor", "ve", M.get(("yev", chunk, grp - 1)))
                    avv = None
                    for tb in range(nb):
                        avv = Sc.op("tensor", lambda t, qh=qh, kvh=kvh, t0=t0, tb=tb, pp=pp: t.matmul(
                            psA[:, qh * 128:(qh + 1) * 128],
                            v_sb[:, t0 // 128 + tb, kvh * 128:(kvh + 1) * 128],
                            attnT_b[:, pp, tb, :],
                            start=(tb == 0), stop=(tb == nb - 1)),
                            inc="pe" if tb == nb - 1 else None)
                    M[("avv", chunk, g2)] = avv
                    if qh == 3:
                        Sc.wait("vector", "pe", avv)
                        M[("yev", chunk, grp)] = Sc.op(
                            "vector", lambda v, kvh=kvh, qbi=qbi: v.tensor_copy(
                                yT_sb[:, kvh * 4:(kvh + 1) * 4, qbi * 128:(qbi + 1) * 128],
                                psA[:, 0:512].rearrange("p (j f) -> p j f", j=4)), inc="ve")

                emit_scores(0)
                emit_stage1(0)
                emit_scores(1)
                for g2 in range(NG):
                    emit_stage2(g2)
                    if g2 + 1 < NG:
                        emit_stage1(g2 + 1)
                    if g2 + 2 < NG:
                        emit_scores(g2 + 2)
                    emit_ptav(g2, g2 // 4)
                Sc.wait("tensor", "ac", Sc.current("ac"))
                Sc.wait("tensor", "ve", Sc.current("ve"))
                Sc.wait("sync", "ve", Sc.current("ve"))
                if dbg == "dy":
                    Sc.dma("gpsimd", out[0:2048, chunk * 512:(chunk + 1) * 512].rearrange(
                        "(g p) s -> p g s", p=128), yT_sb[:, :, :], "gp")
                    if chunk == 3:
                        return
                    continue

                for qbp in range(2):
                    for db in range(NHB):
                        wpar = db % 2
                        Sc.wait("sync", "pe", M.get(("opmm", chunk, qbp, db - 2)))
                        if db < 2 and qbp > 0:
                            Sc.wait("sync", "pe", M.get(("opmm", chunk, qbp - 1, 14 + db)))
                        wld = Sc.dma("sync", wo_row[:, wpar, :], wo_sc[db], "sy")
                        Sc.wait("tensor", "sy", wld)
                        if db == 0 and qbp > 0:
                            Sc.wait("tensor", "ve", M.get(("orv", chunk, qbp - 1, 1)))
                        for j in range(2):
                            ps = psA if j == 0 else psB
                            qb = qbp * 2 + j
                            for n4 in range(4):
                                opv = Sc.op("tensor", lambda t, ps=ps, db=db, qb=qb, n4=n4, wpar=wpar: t.matmul(
                                    ps[:, n4 * 512:(n4 + 1) * 512],
                                    yT_sb[:, db, qb * 128:(qb + 1) * 128],
                                    wo_row[:, wpar, n4 * 512:(n4 + 1) * 512],
                                    start=(db == 0), stop=(db == NHB - 1)),
                                    inc="pe" if (j == 1 and n4 == 3) else None)
                        M[("opmm", chunk, qbp, db)] = Sc.current("pe")
                    for j in range(2):
                        ps = psA if j == 0 else psB
                        sb = chunk * 4 + qbp * 2 + j
                        Sc.wait("sync", "ve", M.get(("orv", chunk, qbp - 1, j)))
                        ld = Sc.dma("sync", xt[:, j, :], x1_sc[sb * 128:(sb + 1) * 128, :], "sy")
                        Sc.wait("vector", "pe", M[("opmm", chunk, qbp, NHB - 1)])
                        Sc.wait("vector", "sy", ld)
                        Sc.wait("vector", "sy", M.get(("rst", chunk, qbp - 1, j)))
                        rv = Sc.op("vector", lambda v, ps=ps, j=j: v.tensor_tensor(
                            res_f[:, j, :], ps[:, :], xt[:, j, :], ALU.add), inc="ve")
                        M[("orv", chunk, qbp, j)] = rv
                        Sc.wait("sync", "ve", rv)
                        M[("rst", chunk, qbp, j)] = Sc.dma(
                            "sync", attn_dst[sb * 128:(sb + 1) * 128, :], res_f[:, j, :], "sy")
            Sc.barrier()

        if "dln" in parts:
            for sb in range(NSB):
                par, norm = emit_ln_tile_chained(x0, sb, "dln")
                Sc.wait("sync", "ve", norm)
                stv = Sc.dma("sync", out[sb * 128:(sb + 1) * 128, :], xn_f[:], "sy")
                lnst["xnf_free"][("dln", sb)] = None
                Sc.wait("scalar", "sy", stv)
        if "dxnt" in parts:
            for sb in range(NSB):
                emit_ln_tile_chained(x0, sb, "dxnt")
                Sc.wait("tensor", "ve", lnst["norm"][("dxnt", sb)])
                pev, _ = emit_transposes(
                    xn_f, 16,
                    lambda g, cnt, sb=sb: xnT[:, g * 8:g * 8 + cnt, sb * 128:(sb + 1) * 128])
                lnst["xnf_free"][("dxnt", sb)] = ("pe", pev)
            Sc.barrier()
            Sc.dma("gpsimd", out.rearrange("(hb p) s -> p hb s", p=128), xnT[:, :, :], "gp")
        at_dbg_probe = any(q in parts for q in ("dk", "dv", "dq", "dy"))
        if "f1" in parts:
            use_x1 = ("at" in parts or "f2" in parts or at_dbg_probe)
            emit_ffn(1, x0, x1_sc if use_x1 else out,
                     a1_stats=("at" in parts or at_dbg_probe))
        at_dbg = None
        for pp_ in parts:
            if pp_.startswith("d") and pp_ in ("dk", "dv", "dq", "dy"):
                at_dbg = pp_
        if "at" in parts or at_dbg:
            emit_attn(x2_sc if "f2" in parts else out, dbg=at_dbg)
        if "f2" in parts:
            emit_ffn(2, x2_sc if "at" in parts else x1_sc, out)

        Sc.wait("sync", "sy", Sc.current("sy"))
        Sc.wait("sync", "sq", Sc.current("sq"))
        Sc.wait("sync", "gp", Sc.current("gp"))

        Sc.check()
        blk = es.enter_context(nc.Block())
        Sc.emit(blk, sem_handles)
    return nc


def kernel(L0_attn_x, L0_attn_Wq, L0_attn_Wk, L0_attn_Wv, L0_attn_Wo,
           L0_attn_q_gain, L0_ffn_gate_w, L0_ffn_gate_scale, L0_ffn_up_w,
           L0_ffn_up_scale, L0_ffn_down_w, L0_ffn_down_scale):
    from concourse.bass_utils import run_bass_kernel_spmd

    x = np.ascontiguousarray(np.asarray(L0_attn_x, dtype=np.float32))
    B = x.shape[0]

    half = 64
    inv_freq = (10000.0 ** (-(np.arange(0, half, dtype=np.float32)) / half)).astype(np.float32)
    freqs = np.arange(S, dtype=np.float32)[:, None] * inv_freq[None, :]
    c_cos = np.ascontiguousarray(
        np.broadcast_to(np.cos(freqs).astype(np.float32)[:, None, :], (S, H, half)))
    c_sin = np.ascontiguousarray(
        np.broadcast_to(np.sin(freqs).astype(np.float32)[:, None, :], (S, H, half)))
    mask = np.full((128, 640), -1e30, dtype=np.float32)
    for i in range(128):
        mask[i, i + 1:i + 513] = 0.0

    shared = {
        "wq_t": np.ascontiguousarray(np.asarray(L0_attn_Wq, dtype=np.float32).T),
        "wk_t": np.ascontiguousarray(np.asarray(L0_attn_Wk, dtype=np.float32).T),
        "wv_t": np.ascontiguousarray(np.asarray(L0_attn_Wv, dtype=np.float32).T),
        "wo_t": np.ascontiguousarray(np.asarray(L0_attn_Wo, dtype=np.float32).T),
        "q_gain": np.ascontiguousarray(np.asarray(L0_attn_q_gain, dtype=np.float32).reshape(1, H)),
        "gw_t": np.ascontiguousarray(np.asarray(L0_ffn_gate_w, dtype=np.int32).T),
        "uw_t": np.ascontiguousarray(np.asarray(L0_ffn_up_w, dtype=np.int32).T),
        "dw_t": np.ascontiguousarray(np.asarray(L0_ffn_down_w, dtype=np.int32).T),
        "g_sc": np.ascontiguousarray(np.asarray(L0_ffn_gate_scale, dtype=np.float32).reshape(NIB, 128).T),
        "u_sc": np.ascontiguousarray(np.asarray(L0_ffn_up_scale, dtype=np.float32).reshape(NIB, 128).T),
        "d_sc": np.ascontiguousarray(np.asarray(L0_ffn_down_scale, dtype=np.float32).reshape(1, D)),
        "c_ident": np.eye(128, dtype=np.float32),
        "c_cos": c_cos,
        "c_sin": c_sin,
        "c_mask": mask,
    }
    in_maps = [dict(shared, x0=np.ascontiguousarray(x[c])) for c in range(B)]

    nc = build_program(parts=("f1", "at", "f2"))
    res = run_bass_kernel_spmd(nc, in_maps, core_ids=list(range(B)))
    outs = [res.results[c]["out"] for c in range(B)]
    return np.stack(outs, axis=0).astype(np.float32)


# version 21
# speedup vs baseline: 1.0556x; 1.0556x over previous
import contextlib
import numpy as np

import concourse.bass as bass
import concourse.mybir as mybir

F32 = mybir.dt.float32
BF16 = mybir.dt.bfloat16
I32 = mybir.dt.int32
AF = mybir.ActivationFunctionType
ALU = mybir.AluOpType
AX = mybir.AxisListType

S = 2048
D = 2048
IF = 8192
NSB = 16
NHB = 16
NIB = 64
H = 16
KVH = 4
HD = 128
EPS_LN = 1e-5
EPS_RMS = 1.1920928955078125e-07
INVSQRT_HD = 1.0 / float(np.sqrt(128.0))

ENGS = ("tensor", "vector", "scalar", "sync", "gpsimd")


class Script:
    def __init__(self):
        self.items = []
        self.sem_count = {}

    def _sem(self, key):
        self.sem_count.setdefault(key, 0)
        return key

    def op(self, eng, fn, inc=None, inc_amt=1):
        val = None
        if inc is not None:
            self._sem(inc)
            self.sem_count[inc] += inc_amt
            val = self.sem_count[inc]
        self.items.append((eng, "op", (fn, inc, inc_amt)))
        return val

    def dma(self, eng, out, in_, inc, **kw):
        self._sem(inc)
        if eng == "gpsimd":
            self.wait(eng, inc, self.sem_count[inc])
        self.sem_count[inc] += 16
        val = self.sem_count[inc]

        def fn(h, out=out, in_=in_, kw=kw):
            return h.dma_start(out=out, in_=in_, **kw)

        self.items.append((eng, "op", (fn, inc, 16)))
        return val

    def wait(self, eng, key, val):
        if val is None or (isinstance(val, int) and val <= 0):
            return
        self._sem(key)
        self.items.append((eng, "wait", (key, val)))

    def barrier(self, engines=ENGS, exclude=()):
        snap = dict(self.sem_count)
        for e in engines:
            for key, val in snap.items():
                if val > 0 and key not in exclude:
                    self.items.append((e, "wait", (key, val)))

    def current(self, key):
        return self.sem_count.get(key, 0)

    def fence(self, eng, key):
        self.wait(eng, key, self.current(key))

    def check(self):
        queues = {e: [] for e in ENGS}
        for eng, kind, payload in self.items:
            queues[eng].append((kind, payload))
        pos = {e: 0 for e in ENGS}
        sems = {k: 0 for k in self.sem_count}
        progress = True
        while progress:
            progress = False
            for e in ENGS:
                while pos[e] < len(queues[e]):
                    kind, payload = queues[e][pos[e]]
                    if kind == "wait":
                        key, val = payload
                        if sems.get(key, 0) < val:
                            break
                    else:
                        _, inc, amt = payload
                        if inc is not None:
                            sems[inc] += amt
                    pos[e] += 1
                    progress = True
        stuck = {e: (pos[e], len(queues[e])) for e in ENGS if pos[e] < len(queues[e])}
        if stuck:
            det = []
            for e, (p, n) in stuck.items():
                kind, payload = queues[e][p]
                det.append(f"{e}@{p}/{n}:{kind}:{payload if kind == 'wait' else ''}")
            raise RuntimeError(f"script deadlock: {det}")

    def emit(self, blk, sem_handles):
        def make_body(eng):
            def body(h):
                for e, kind, payload in self.items:
                    if e != eng:
                        continue
                    if kind == "wait":
                        key, val = payload
                        h.wait_ge(sem_handles[key], val)
                    else:
                        fn, inc, amt = payload
                        ins = fn(h)
                        if inc is not None:
                            ins.then_inc(sem_handles[inc], amt)
            return body

        blk.tensor(make_body("tensor"))
        blk.vector(make_body("vector"))
        blk.scalar(make_body("scalar"))
        blk.sync(make_body("sync"))
        blk.gpsimd(make_body("gpsimd"))


def build_program(parts=("f1", "at", "f2")):
    nc = bass.Bass("TRN2", target_bir_lowering=False)

    x0 = nc.dram_tensor("x0", [S, D], F32, kind="ExternalInput").ap()
    wq_t = nc.dram_tensor("wq_t", [D, D], F32, kind="ExternalInput").ap()
    wk_t = nc.dram_tensor("wk_t", [D, KVH * HD], F32, kind="ExternalInput").ap()
    wv_t = nc.dram_tensor("wv_t", [D, KVH * HD], F32, kind="ExternalInput").ap()
    wo_t = nc.dram_tensor("wo_t", [D, D], F32, kind="ExternalInput").ap()
    q_gain = nc.dram_tensor("q_gain", [1, H], F32, kind="ExternalInput").ap()
    gw_t = nc.dram_tensor("gw_t", [D, IF], I32, kind="ExternalInput").ap()
    uw_t = nc.dram_tensor("uw_t", [D, IF], I32, kind="ExternalInput").ap()
    dw_t = nc.dram_tensor("dw_t", [IF, D], I32, kind="ExternalInput").ap()
    g_sc = nc.dram_tensor("g_sc", [128, NIB], F32, kind="ExternalInput").ap()
    u_sc = nc.dram_tensor("u_sc", [128, NIB], F32, kind="ExternalInput").ap()
    d_sc = nc.dram_tensor("d_sc", [1, D], F32, kind="ExternalInput").ap()
    c_ident = nc.dram_tensor("c_ident", [128, 128], F32, kind="ExternalInput").ap()
    c_cos = nc.dram_tensor("c_cos", [S, H, 64], F32, kind="ExternalInput").ap()
    c_sin = nc.dram_tensor("c_sin", [S, H, 64], F32, kind="ExternalInput").ap()
    c_mask = nc.dram_tensor("c_mask", [128, 640], F32, kind="ExternalInput").ap()

    out = nc.dram_tensor("out", [S, D], F32, kind="ExternalOutput").ap()

    x1_sc = nc.dram_tensor("x1_sc", [S, D], F32, kind="Internal").ap()
    x2_sc = nc.dram_tensor("x2_sc", [S, D], F32, kind="Internal").ap()
    gw_sc = nc.dram_tensor("gw_sc", [NIB, 128, D], BF16, kind="Internal").ap()
    uw_sc = nc.dram_tensor("uw_sc", [NIB, 128, D], BF16, kind="Internal").ap()
    dw_sc = nc.dram_tensor("dw_sc", [NIB, 128, D], BF16, kind="Internal").ap()
    wo_sc = nc.dram_tensor("wo_sc", [NHB, 128, D], BF16, kind="Internal").ap()
    h2_sc = nc.dram_tensor("h2_sc", [NIB, 128, S], BF16, kind="Internal").ap()

    es = contextlib.ExitStack()
    with es:
        KB = 1024

        def alloc(name, shape, dt):
            return nc.alloc_sbuf_tensor(name, shape, dt)

        xt = alloc("xt", [128, 2, D], F32)
        res_f = alloc("res_f", [128, 2, D], F32)
        ident = alloc("ident", [128, 128], F32)
        identb = alloc("identb", [128, 128], BF16)
        ones1 = alloc("ones1", [1, 128], F32)
        gainrow = alloc("gainrow", [1, H], F32)
        gain_bc = alloc("gain_bc", [128, H], F32)
        eps_col = alloc("eps_col", [128, 1], F32)
        nhalf_col = alloc("nhalf_col", [128, 1], F32)
        gs_col = alloc("gs_col", [128, NIB], F32)
        us_col = alloc("us_col", [128, NIB], F32)
        st_mean = alloc("st_mean", [128, 16], F32)
        st_s2 = alloc("st_s2", [128, 16], F32)
        st_t1 = alloc("st_t1", [128, 16], F32)
        st_var = alloc("st_var", [128, 16], F32)
        st_rstd = alloc("st_rstd", [128, 16], F32)
        st_s2b = alloc("st_s2b", [128, 16], F32)
        st_mnb = alloc("st_mnb", [128, 16], F32)
        sk_a = alloc("sk_a", [128, 4], F32)
        sk_b = alloc("sk_b", [128, 4], F32)
        sq_a = alloc("sq_a", [128, 16], F32)
        sq_b = alloc("sq_b", [128, 16], F32)
        nmx_c = alloc("nmx_c", [128, 4], F32)
        z_c = alloc("z_c", [128, 4], F32)
        rz_c = alloc("rz_c", [128, 4], F32)
        mx_c = alloc("mx_c", [128, 4], F32)

        bigA = nc.alloc_sbuf_tensor("bigA", [128, 16, 2048], BF16)
        xnT = bigA.ap()
        ring_h2 = bigA.ap()
        wq_sb = bigA.ap()

        bigB = nc.alloc_sbuf_tensor("bigB", [128, 16, 1024], F32)
        B_addr = nc.lookup_mloc(bigB).addr
        xacc = bigB.ap()
        kT_sb = nc.alloc_sbuf_tensor_at("kT_sb", [128, KVH, S], BF16, offset=B_addr).ap()
        v_sb = nc.alloc_sbuf_tensor_at("v_sb", [128, NSB, 512], BF16, offset=B_addr + 16 * KB).ap()
        qT_sb = nc.alloc_sbuf_tensor_at("qT_sb", [128, H, 512], BF16, offset=B_addr + 32 * KB).ap()
        kv_wv = nc.alloc_sbuf_tensor_at("kv_wv", [128, NHB, 512], BF16, offset=B_addr + 32 * KB).ap()
        yT_sb = nc.alloc_sbuf_tensor_at("yT_sb", [128, H, 512], BF16, offset=B_addr + 48 * KB).ap()
        kstage = nc.alloc_sbuf_tensor_at("kstage", [128, 1024], F32, offset=B_addr + 48 * KB).ap()
        kvrow = nc.alloc_sbuf_tensor_at("kvrow", [128, 512], F32, offset=B_addr + 52 * KB).ap()
        qrot2 = nc.alloc_sbuf_tensor_at("qrot2", [128, D], BF16, offset=B_addr + 48 * KB).ap()

        bigC = nc.alloc_sbuf_tensor("bigC", [128, 45 * KB], mybir.dt.uint8)
        C = nc.lookup_mloc(bigC).addr
        at_ = nc.alloc_sbuf_tensor_at
        wgu = at_("wgu", [128, 2, 2, NHB, 128], BF16, offset=C).ap()
        h2t = at_("h2t", [128, 2, D], BF16, offset=C + 16 * KB).ap()
        silu_b = at_("silu_b", [128, 2, 512], F32, offset=C + 24 * KB).ap()
        ring_dw = at_("ring_dw", [128, 16, 1024], BF16, offset=C).ap()
        ds_bc = at_("ds_bc", [128, D], F32, offset=C + 32 * KB).ap()
        ctile = at_("ctile", [128, H, 64], F32, offset=C + 21 * KB).ap()
        stile = at_("stile", [128, H, 64], F32, offset=C + 25 * KB).ap()
        xn_f = at_("xn_f", [128, D], F32, offset=C + 29 * KB).ap()
        xnT1c = at_("xnT1c", [128, 2, NHB, 128], BF16, offset=C + 37 * KB).ap()
        kv_wk = at_("kv_wk", [128, NHB, 512], BF16, offset=C).ap()


        krot = at_("krot", [128, 2, 512], BF16, offset=C + 16 * KB).ap()
        qrot = at_("qrot", [128, D], BF16, offset=C + 8 * KB).ap()
        qcross = at_("qcross", [128, H, 128], F32, offset=C + 32 * KB).ap()
        xn_b = at_("xn_b", [128, D], BF16, offset=C + 29 * KB).ap()
        probs_b = at_("probs_b", [128, 3, 640], BF16, offset=C + 41 * KB).ap()
        wo_row = at_("wo_row", [128, 2, D], BF16, offset=C).ap()
        stdump = at_("stdump", [128, 1024], F32, offset=C + 40 * KB).ap()
        probs_f = at_("probs_f", [128, 3, 640], F32, offset=C + 8 * KB).ap()
        attnT_b = at_("attnT_b", [128, 2, 5, 128], BF16, offset=C + 15 * KB + 512).ap()
        mask_sb = at_("mask_sb", [128, 640], F32, offset=C + 18 * KB).ap()

        psA = nc.alloc_psum_tensor("psA", [128, 2048], F32).ap()
        psB = nc.alloc_psum_tensor("psB", [128, 2048], F32).ap()

        sem_handles = {k: es.enter_context(nc.semaphore(k))
                       for k in ("sy", "sq", "gp", "pe", "ve", "ac")}

        Sc = Script()
        M = {}

        Sc.dma("sync", ident[:], c_ident, "sy")
        Sc.dma("sync", gs_col[:], g_sc, "sy")
        Sc.dma("sync", us_col[:], u_sc, "sy")
        Sc.dma("sync", gainrow[:], q_gain, "sy")
        Sc.dma("sync", mask_sb[:], c_mask, "sy")
        M["const_sy"] = Sc.current("sy")
        cbv = Sc.dma("gpsimd", identb[:], c_ident, "gp")
        Sc.wait("vector", "gp", cbv)
        Sc.wait("vector", "sy", M["const_sy"])
        Sc.op("vector", lambda v: v.memset(ones1[:], 1.0), inc="ve")
        Sc.op("vector", lambda v: v.memset(eps_col[:], EPS_LN), inc="ve")
        Sc.op("vector", lambda v: v.memset(nhalf_col[:], -0.5), inc="ve")
        Sc.barrier()

        lnst = {"norm": {}, "sq": {}, "xnf_free": {}, "rstd": {}}
        tp_evac = {}

        def emit_ln_stats(xsrc, tag, buf=None, dump=None, gate=None):
            buf = xt if buf is None else buf
            dump = xn_f if dump is None else dump
            for sb in range(NSB):
                par = sb % 2
                Sc.wait("sync", "ac", lnst["sq"].get((tag, "st", sb - 2)))
                Sc.wait("sync", "ve", lnst["sq"].get((tag, "sv", sb - 2)))
                if gate is not None:
                    Sc.wait("sync", "sy", gate(sb))
                ld = Sc.dma("sync", buf[:, par, :], xsrc[sb * 128:(sb + 1) * 128, :], "sy")
                Sc.wait("scalar", "sy", ld)
                lnst["sq"][(tag, "st", sb)] = Sc.op(
                    "scalar", lambda s, par=par, sb=sb, buf=buf, dump=dump: s.activation(
                        dump[:], buf[:, par, :], AF.Square,
                        accum_out=st_s2[:, sb:sb + 1]), inc="ac")
                Sc.wait("vector", "sy", ld)
                lnst["sq"][(tag, "sv", sb)] = Sc.op(
                    "vector", lambda v, par=par, sb=sb, buf=buf: v.reduce_sum(
                        st_mean[:, sb:sb + 1], buf[:, par, :], axis=AX.X), inc="ve")
            Sc.fence("vector", "ve")
            Sc.op("vector", lambda v: v.tensor_scalar(
                st_mean[:, :], st_mean[:, :], 1.0 / D, None, ALU.mult), inc="ve")
            Sc.fence("vector", "ve")
            Sc.op("vector", lambda v: v.tensor_tensor(
                st_t1[:, :], st_mean[:, :], st_mean[:, :], ALU.mult), inc="ve")
            Sc.fence("vector", "ve")
            Sc.wait("vector", "ac", lnst["sq"][(tag, "st", NSB - 1)])
            vd = Sc.op("vector", lambda v: v.scalar_tensor_tensor(
                st_var[:, :], st_s2[:, :], 1.0 / D, st_t1[:, :],
                ALU.mult, ALU.subtract), inc="ve")
            Sc.wait("scalar", "ve", vd)
            Sc.op("scalar", lambda s: s.activation(
                st_t1[:, :], st_var[:, :], AF.Ln, bias=eps_col[:]), inc="ac")
            rstd = Sc.op("scalar", lambda s: s.activation(
                st_rstd[:, :], st_t1[:, :], AF.Exp, scale=nhalf_col[:]), inc="ac")
            lnst["rstd"][tag] = rstd

        def emit_ln_tile(xsrc, sb, tag, stats_tag=None):
            par = sb % 2
            key = (tag, sb - 2)
            Sc.wait("sync", "ve", lnst["norm"].get(key))
            if sb < 2:
                Sc.wait("sync", "ac", lnst["rstd"][stats_tag or tag])
            ld = Sc.dma("sync", xt[:, par, :], xsrc[sb * 128:(sb + 1) * 128, :], "sy")
            c = sb % 16
            Sc.wait("vector", "sy", ld)
            Sc.wait("vector", "ac", lnst["rstd"][stats_tag or tag])
            xf = lnst["xnf_free"].get((tag, sb - 1))
            if xf is not None:
                Sc.wait("vector", xf[0], xf[1])
            norm = Sc.op("vector", lambda v, par=par, c=c: v.tensor_scalar(
                xn_b[:], xt[:, par, :], st_mean[:, c:c + 1], st_rstd[:, c:c + 1],
                ALU.subtract, ALU.mult), inc="ve")
            lnst["norm"][(tag, sb)] = norm
            return par, norm

        def emit_ln_tile_chained(xsrc, sb, tag):
            par = sb % 2
            key = (tag, sb - 2)
            Sc.wait("sync", "ve", lnst["norm"].get(key))
            Sc.wait("sync", "ac", lnst["sq"].get(key))
            ld = Sc.dma("sync", xt[:, par, :], xsrc[sb * 128:(sb + 1) * 128, :], "sy")
            c = sb % 16
            Sc.wait("scalar", "sy", ld)
            xf = lnst["xnf_free"].get((tag, sb - 1))
            if xf is not None:
                Sc.wait("scalar", xf[0], xf[1])
            sq = Sc.op("scalar", lambda s, par=par, c=c: s.activation(
                xn_f[:], xt[:, par, :], AF.Square, accum_out=st_s2[:, c:c + 1]), inc="ac")
            lnst["sq"][(tag, sb)] = sq
            Sc.wait("vector", "sy", ld)
            Sc.op("vector", lambda v, par=par, c=c: v.reduce_sum(
                st_mean[:, c:c + 1], xt[:, par, :], axis=AX.X), inc="ve")
            Sc.fence("vector", "ve")
            mean_done = Sc.op("vector", lambda v, c=c: v.tensor_scalar(
                st_mean[:, c:c + 1], st_mean[:, c:c + 1], 1.0 / D, None, ALU.mult), inc="ve")
            Sc.fence("vector", "ve")
            Sc.op("vector", lambda v, c=c: v.tensor_tensor(
                st_t1[:, c:c + 1], st_mean[:, c:c + 1], st_mean[:, c:c + 1], ALU.mult), inc="ve")
            Sc.fence("vector", "ve")
            Sc.wait("vector", "ac", sq)
            var_done = Sc.op("vector", lambda v, c=c: v.scalar_tensor_tensor(
                st_var[:, c:c + 1], st_s2[:, c:c + 1], 1.0 / D, st_t1[:, c:c + 1],
                ALU.mult, ALU.subtract), inc="ve")
            Sc.wait("scalar", "ve", var_done)
            Sc.op("scalar", lambda s, c=c: s.activation(
                st_t1[:, c:c + 1], st_var[:, c:c + 1], AF.Ln, bias=eps_col[:]), inc="ac")
            rstd = Sc.op("scalar", lambda s, c=c: s.activation(
                st_rstd[:, c:c + 1], st_t1[:, c:c + 1], AF.Exp, scale=nhalf_col[:]), inc="ac")
            Sc.wait("vector", "ac", rstd)
            Sc.wait("vector", "ve", mean_done)
            norm = Sc.op("vector", lambda v, par=par, c=c: v.tensor_scalar(
                xn_b[:], xt[:, par, :], st_mean[:, c:c + 1], st_rstd[:, c:c + 1],
                ALU.subtract, ALU.mult), inc="ve")
            lnst["norm"][(tag, sb)] = norm
            return par, norm

        def emit_transposes(src_ap, n_tiles, dest_fn, src_off=0, evac_extra_war=None):
            last_pe = last_ve = None
            j0 = 0
            g = 0
            while j0 < n_tiles:
                cnt = min(8, n_tiles - j0)
                rot = emit_transposes.rot
                emit_transposes.rot ^= 1
                Sc.wait("tensor", "ve", tp_evac.get(rot))
                for j in range(cnt):
                    pev = Sc.op("tensor", lambda t, rot=rot, j=j, jj=j0 + j: t.matmul(
                        psB[:, rot * 1024 + j * 128: rot * 1024 + (j + 1) * 128],
                        src_ap[:, src_off + jj * 128: src_off + (jj + 1) * 128], identb[:],
                        start=True, stop=True),
                        inc="pe" if j == cnt - 1 else None)
                Sc.wait("vector", "pe", pev)
                if evac_extra_war is not None:
                    Sc.wait("vector", evac_extra_war[0], evac_extra_war[1])
                ev = Sc.op("vector", lambda v, rot=rot, g=g, cnt=cnt: v.tensor_copy(
                    dest_fn(g, cnt),
                    psB[:, rot * 1024: rot * 1024 + cnt * 128].rearrange(
                        "p (j f) -> p j f", j=cnt)), inc="ve")
                tp_evac[rot] = ev
                last_pe, last_ve = pev, ev
                j0 += cnt
                g += 1
            return last_pe, last_ve
        emit_transposes.rot = 0

        def emit_ffn(call, xsrc, xdst, a1_stats=False):
            tag = f"f{call}"
            for sb in range(NSB):
                emit_ln_tile_chained(xsrc, sb, tag)
                Sc.wait("tensor", "ve", lnst["norm"][(tag, sb)])
                pev, _ = emit_transposes(
                    xn_b, 16,
                    lambda g, cnt, sb=sb: xnT[:, g * 8:g * 8 + cnt, sb * 128:(sb + 1) * 128])
                lnst["xnf_free"][(tag, sb)] = ("pe", pev)
            Sc.barrier()

            mmdone = {}
            h2qd = {}
            h2store = {}
            for ib in range(NIB):
                par = ib % 2
                war_pe = mmdone.get(ib - 2)
                if call == 1:
                    Sc.wait("gpsimd", "pe", war_pe)
                    Sc.wait("gpsimd", "sq", M.get((tag, "wst", ib - 2)))
                    Sc.dma("gpsimd", wgu[:, par, 0, :, :],
                           gw_t[:, ib * 128:(ib + 1) * 128].rearrange("(hb p) i -> p hb i", p=128), "gp")
                    g2 = Sc.dma("gpsimd", wgu[:, par, 1, :, :],
                                uw_t[:, ib * 128:(ib + 1) * 128].rearrange("(hb p) i -> p hb i", p=128), "gp")
                    Sc.wait("vector", "gp", g2)
                    Sc.op("vector", lambda v, par=par: v.tensor_scalar(
                        wgu[:, par, 0].rearrange("p a b -> p (a b)"),
                        wgu[:, par, 0].rearrange("p a b -> p (a b)"),
                        -128.0, None, ALU.add), inc="ve")
                    wv = Sc.op("vector", lambda v, par=par: v.tensor_scalar(
                        wgu[:, par, 1].rearrange("p a b -> p (a b)"),
                        wgu[:, par, 1].rearrange("p a b -> p (a b)"),
                        -128.0, None, ALU.add), inc="ve")
                    wrdy = ("ve", wv)
                    Sc.wait("scalar", "ve", wv)
                    Sc.dma("scalar", gw_sc[ib], wgu[:, par, 0].rearrange("p a b -> p (a b)"), "sq")
                    M[(tag, "wst", ib)] = Sc.dma(
                        "scalar", uw_sc[ib], wgu[:, par, 1].rearrange("p a b -> p (a b)"), "sq")
                else:
                    Sc.wait("scalar", "pe", war_pe)
                    Sc.dma("scalar", wgu[:, par, 0].rearrange("p a b -> p (a b)"), gw_sc[ib], "sq")
                    wv = Sc.dma("scalar", wgu[:, par, 1].rearrange("p a b -> p (a b)"), uw_sc[ib], "sq")
                    wrdy = ("sq", wv)
                for q4 in range(4):
                    gq = ib * 4 + q4
                    qpar = gq % 2
                    if q4 == 0:
                        Sc.wait("tensor", wrdy[0], wrdy[1])
                    Sc.wait("tensor", "ve", h2qd.get(gq - 2))
                    for hb in range(NHB):
                        gv = Sc.op("tensor", lambda t, par=par, qpar=qpar, hb=hb, q4=q4: t.matmul(
                            psA[:, qpar * 512:(qpar + 1) * 512],
                            wgu[:, par, 0, hb, :],
                            xnT[:, hb, q4 * 512:(q4 + 1) * 512],
                            start=(hb == 0), stop=(hb == NHB - 1)),
                            inc="pe" if hb == NHB - 1 else None)
                    for hb in range(NHB):
                        uv = Sc.op("tensor", lambda t, par=par, qpar=qpar, hb=hb, q4=q4: t.matmul(
                            psA[:, 1024 + qpar * 512:1024 + (qpar + 1) * 512],
                            wgu[:, par, 1, hb, :],
                            xnT[:, hb, q4 * 512:(q4 + 1) * 512],
                            start=(hb == 0), stop=(hb == NHB - 1)),
                            inc="pe" if hb == NHB - 1 else None)
                    Sc.wait("scalar", "pe", gv)
                    Sc.wait("scalar", "ve", h2qd.get(gq - 2))
                    sil = Sc.op("scalar", lambda s, qpar=qpar, ib=ib: s.activation(
                        silu_b[:, qpar, :], psA[:, qpar * 512:(qpar + 1) * 512], AF.Silu,
                        scale=gs_col[:, ib:ib + 1]), inc="ac")
                    Sc.wait("vector", "pe", uv)
                    Sc.wait("vector", "ac", sil)
                    if q4 == 0:
                        Sc.wait("vector", "sy", h2store.get(ib - 2))
                    h2qd[gq] = Sc.op("vector", lambda v, par=par, qpar=qpar, ib=ib, q4=q4: v.scalar_tensor_tensor(
                        h2t[:, par, q4 * 512:(q4 + 1) * 512],
                        psA[:, 1024 + qpar * 512:1024 + (qpar + 1) * 512],
                        us_col[:, ib:ib + 1],
                        silu_b[:, qpar, :], ALU.mult, ALU.mult), inc="ve")
                mmdone[ib] = Sc.current("pe")
                Sc.wait("sync", "ve", h2qd[ib * 4 + 3])
                h2store[ib] = Sc.dma("sync", h2_sc[ib], h2t[:, par, :], "sy")
            Sc.barrier()

            if call == 1:
                dsr = Sc.dma("sync", ds_bc[0:1, :], d_sc, "sy")
                Sc.wait("tensor", "sy", dsr)
                for j in range(4):
                    bcv = Sc.op("tensor", lambda t, j=j: t.matmul(
                        psB[:, j * 512:(j + 1) * 512], ones1[:], ds_bc[0:1, j * 512:(j + 1) * 512],
                        start=True, stop=True), inc="pe" if j == 3 else None)
                Sc.wait("vector", "pe", bcv)
                dsv = Sc.op("vector", lambda v: v.tensor_scalar(
                    ds_bc[:], psB[:, :], 0.5, None, ALU.mult), inc="ve")
                Sc.wait("tensor", "ve", dsv)

            psum_war = {}
            chunk_mm = {}
            resd_last = 0
            NCH = 8
            KCH = 8
            for hh in range(2):
                for ch0 in range(NCH):
                    ch = hh * NCH + ch0
                    base = (ch % 2) * (16 - KCH)
                    rdy_sy = rdy_ve = rdy_sq = 0
                    for k8 in range(KCH):
                        k = base + k8
                        ib = ch0 * KCH + k8
                        gib = ch * KCH + k8
                        war = chunk_mm.get(ch - 2)
                        Sc.wait("sync", "pe", war)
                        rdy_sy = Sc.dma("sync", ring_h2[:, k, :], h2_sc[ib], "sy")
                        if call == 1:
                            Sc.wait("gpsimd", "pe", war)
                            Sc.wait("gpsimd", "sq", M.get((tag, "dwst", gib - 2 * KCH)))
                            cd = Sc.dma("gpsimd", ring_dw[:, k, :],
                                        dw_t[ib * 128:(ib + 1) * 128, hh * 1024:(hh + 1) * 1024], "gp")
                            Sc.wait("vector", "gp", cd)
                            Sc.op("vector", lambda v, k=k: v.tensor_scalar(
                                ring_dw[:, k, :], ring_dw[:, k, :], -128.0, None, ALU.add), inc="ve")
                            rdy_ve = Sc.op("vector", lambda v, k=k, hh=hh: v.tensor_tensor(
                                ring_dw[:, k, :], ring_dw[:, k, :],
                                ds_bc[:, hh * 1024:(hh + 1) * 1024], ALU.mult), inc="ve")
                            Sc.wait("scalar", "ve", rdy_ve)
                            M[(tag, "dwst", gib)] = Sc.dma(
                                "scalar", dw_sc[ib][:, hh * 1024:(hh + 1) * 1024], ring_dw[:, k, :], "sq")

                        else:
                            Sc.wait("scalar", "pe", war)
                            rdy_sq = Sc.dma("scalar", ring_dw[:, k, :],
                                            dw_sc[ib][:, hh * 1024:(hh + 1) * 1024], "sq")
                    dwkey = "ve" if call == 1 else "sq"
                    Sc.wait("tensor", "sy", rdy_sy)
                    Sc.wait("tensor", dwkey, rdy_ve if call == 1 else rdy_sq)
                    for sb in range(NSB):
                        spar = sb % 2
                        ps = psA if spar == 0 else psB
                        Sc.wait("tensor", "ve", psum_war.get(spar))
                        for k8 in range(KCH):
                            k = base + k8
                            for n2 in range(2):
                                pv = Sc.op("tensor", lambda t, ps=ps, k=k, n2=n2, sb=sb, k8=k8: t.matmul(
                                    ps[:, n2 * 512:(n2 + 1) * 512],
                                    ring_h2[:, k, sb * 128:(sb + 1) * 128],
                                    ring_dw[:, k, n2 * 512:(n2 + 1) * 512],
                                    start=(k8 == 0), stop=(k8 == KCH - 1)),
                                    inc="pe" if (k8 == KCH - 1 and n2 == 1) else None)
                        Sc.wait("vector", "pe", pv)
                        if ch0 == 0:
                            if hh == 1:
                                Sc.wait("vector", "sy", M.get((tag, 0, "hst"), 0))
                                if a1_stats:
                                    Sc.wait("vector", "ac", M[(tag, 0, "sq", sb)])
                            av = Sc.op("vector", lambda v, ps=ps, sb=sb: v.tensor_copy(
                                xacc[:, sb, :], ps[:, 0:1024]), inc="ve")
                        else:
                            av = Sc.op("vector", lambda v, ps=ps, sb=sb: v.tensor_tensor(
                                xacc[:, sb, :], xacc[:, sb, :], ps[:, 0:1024], ALU.add), inc="ve")
                        psum_war[spar] = av
                    chunk_mm[ch] = Sc.current("pe")
                resd = {}
                stv = 0
                for sb in range(NSB):
                    par = sb % 2
                    Sc.wait("sync", "ve", resd.get(sb - 2))
                    if sb < 2:
                        Sc.wait("sync", "ve", resd_last)
                    ld = Sc.dma("sync", xt[:, par, 0:1024],
                                xsrc[sb * 128:(sb + 1) * 128, hh * 1024:(hh + 1) * 1024], "sy")
                    Sc.wait("vector", "sy", ld)
                    rv = Sc.op("vector", lambda v, sb=sb, par=par: v.tensor_tensor(
                        xacc[:, sb, :], xacc[:, sb, :], xt[:, par, 0:1024], ALU.add), inc="ve")
                    resd[sb] = rv
                    if a1_stats:
                        s2d = st_s2 if hh == 0 else st_s2b
                        mnd = st_mean if hh == 0 else st_mnb
                        Sc.wait("scalar", "ve", rv)
                        Sc.op("scalar", lambda s, sb=sb, s2d=s2d: s.activation(
                            stdump[:], xacc[:, sb, :], AF.Square,
                            accum_out=s2d[:, sb:sb + 1]), inc="ac")
                        M[(tag, hh, "sq", sb)] = Sc.op(
                            "scalar", lambda s, sb=sb, mnd=mnd: s.activation(
                                stdump[:], xacc[:, sb, :], AF.Identity,
                                accum_out=mnd[:, sb:sb + 1]), inc="ac")
                    Sc.wait("sync", "ve", rv)
                    stv = Sc.dma("sync", xdst[sb * 128:(sb + 1) * 128, hh * 1024:(hh + 1) * 1024],
                                 xacc[:, sb, :], "sy")
                    M[(tag, hh, "sb_st", sb)] = stv
                M[(tag, hh, "hst")] = stv
                resd_last = resd[NSB - 1]
            if True:
                if a1_stats:
                    Sc.wait("gpsimd", "pe", chunk_mm[2 * NCH - 2])
                    Sc.wait("gpsimd", "sq", M[(tag, "dwst", 14 * KCH + KCH - 1)])
                    M["kvw1"] = Sc.dma("gpsimd", kv_wk[:, :, :],
                                       wk_t.rearrange("(hb p) e -> p hb e", p=128), "gp")
                    Sc.wait("gpsimd", "sy", M[(tag, 1, "sb_st", 11)])
                    Sc.wait("gpsimd", "ac", M[(tag, 1, "sq", 11)])
                    M["kvw2"] = Sc.dma("gpsimd", kv_wv[:, :, :],
                                       wv_t.rearrange("(hb p) e -> p hb e", p=128), "gp")
                    Sc.wait("gpsimd", "pe", chunk_mm[2 * NCH - 1])
                    M["wqv"] = Sc.dma("gpsimd", wq_sb[:, :, :],
                                      wq_t.rearrange("(hb p) e -> p hb e", p=128), "gp")
                    M["wost"] = Sc.dma(
                        "gpsimd", wo_sc[:, :, :],
                        wo_t.rearrange("(db p) e -> db p e", p=128), "gp")
                    Sc.wait("vector", "ac", M[(tag, 1, "sq", NSB - 1)])
                    Sc.op("vector", lambda v: v.tensor_tensor(
                        st_s2[:, :], st_s2[:, :], st_s2b[:, :], ALU.add), inc="ve")
                    Sc.op("vector", lambda v: v.tensor_tensor(
                        st_mean[:, :], st_mean[:, :], st_mnb[:, :], ALU.add), inc="ve")
                    Sc.fence("vector", "ve")
                    Sc.op("vector", lambda v: v.tensor_scalar(
                        st_mean[:, :], st_mean[:, :], 1.0 / D, None, ALU.mult), inc="ve")
                    Sc.fence("vector", "ve")
                    Sc.op("vector", lambda v: v.tensor_tensor(
                        st_t1[:, :], st_mean[:, :], st_mean[:, :], ALU.mult), inc="ve")
                    Sc.fence("vector", "ve")
                    vd = Sc.op("vector", lambda v: v.scalar_tensor_tensor(
                        st_var[:, :], st_s2[:, :], 1.0 / D, st_t1[:, :],
                        ALU.mult, ALU.subtract), inc="ve")
                    Sc.wait("scalar", "ve", vd)
                    Sc.op("scalar", lambda s: s.activation(
                        st_t1[:, :], st_var[:, :], AF.Ln, bias=eps_col[:]), inc="ac")
                    rstd = Sc.op("scalar", lambda s: s.activation(
                        st_rstd[:, :], st_t1[:, :], AF.Exp, scale=nhalf_col[:]), inc="ac")
                    lnst["rstd"]["a1"] = rstd
                if a1_stats:
                    Sc.barrier(engines=("tensor", "vector", "scalar", "sync"))
                else:
                    Sc.barrier()

        def emit_attn(attn_dst, dbg=None):
            kvw2 = M["kvw2"]
            gb = Sc.op("tensor", lambda t: t.matmul(psB[:, 0:H], ones1[:], gainrow[:],
                                                    start=True, stop=True), inc="pe")
            Sc.wait("vector", "pe", gb)
            Sc.op("vector", lambda v: v.tensor_scalar(
                gain_bc[:], psB[:, 0:H], INVSQRT_HD, None, ALU.mult), inc="ve")

            kvmm = {}
            kevd = {}

            def emit_kv_front(sb):
                par = sb % 2
                emit_ln_tile(x1_sc, sb, "akv", stats_tag="a1")
                Sc.wait("sync", "ve", M.get(("krd", sb - 1)))
                rc = Sc.dma("sync", ctile[:, 0:KVH, :],
                            c_cos[sb * 128:(sb + 1) * 128, 0:KVH, :], "sy")
                rs = Sc.dma("sync", stile[:, 0:KVH, :],
                            c_sin[sb * 128:(sb + 1) * 128, 0:KVH, :], "sy")
                Sc.wait("tensor", "ve", lnst["norm"][("akv", sb)])
                if sb == 0:
                    Sc.wait("vector", "gp", M["kvw2"])
                pev, _ = emit_transposes(
                    xn_b, 16,
                    lambda g, cnt, par=par: xnT1c[:, par, g * 8:g * 8 + cnt, :],
                    evac_extra_war=("pe", kvmm.get(sb - 2)))
                lnst["xnf_free"][("akv", sb)] = ("pe", pev)
                if sb == 0:
                    Sc.wait("tensor", "gp", M["kvw2"])
                kslot = par * 512
                vslot = 1024 + par * 512
                Sc.wait("tensor", "ac", kevd.get(sb - 2))
                for hb in range(NHB):
                    kv_mm = Sc.op("tensor", lambda t, par=par, hb=hb, kslot=kslot: t.matmul(
                        psA[:, kslot:kslot + 512], xnT1c[:, par, hb, :], kv_wk[:, hb, :],
                        start=(hb == 0), stop=(hb == NHB - 1)),
                        inc="pe" if hb == NHB - 1 else None)
                for hb in range(NHB):
                    vv_mm = Sc.op("tensor", lambda t, par=par, hb=hb, vslot=vslot: t.matmul(
                        psA[:, vslot:vslot + 512], xnT1c[:, par, hb, :], kv_wv[:, hb, :],
                        start=(hb == 0), stop=(hb == NHB - 1)),
                        inc="pe" if hb == NHB - 1 else None)
                kvmm[sb] = Sc.current("pe")
                Sc.wait("scalar", "pe", vv_mm)
                Sc.op("scalar", lambda s, sb=sb, vslot=vslot: s.activation(
                    v_sb[:, sb, :], psA[:, vslot:vslot + 512], AF.Identity), inc="ac")
                Sc.wait("scalar", "ve", M.get(("krd", sb - 1)))
                kev = Sc.op("scalar", lambda s, kslot=kslot: s.activation(
                    kvrow[:], psA[:, kslot:kslot + 512], AF.Identity), inc="ac")
                kevd[sb] = kev
                Sc.wait("vector", "ac", kev)
                Sc.op("vector", lambda v: v.tensor_tensor(
                    kstage[:, 0:512], kvrow[:], kvrow[:], ALU.mult), inc="ve")
                Sc.fence("vector", "ve")
                Sc.op("vector", lambda v: v.reduce_sum(
                    sk_a[:, :], kstage[:, 0:512].rearrange("p (g f) -> p g f", g=KVH),
                    axis=AX.X), inc="ve")
                Sc.fence("vector", "ve")
                rms1 = Sc.op("vector", lambda v: v.tensor_scalar(
                    sk_a[:, :], sk_a[:, :], 1.0 / HD, EPS_RMS, ALU.mult, ALU.add), inc="ve")
                Sc.wait("scalar", "ve", rms1)
                Sc.op("scalar", lambda s: s.activation(sk_b[:, :], sk_a[:, :], AF.Ln), inc="ac")
                krst = Sc.op("scalar", lambda s: s.activation(
                    sk_a[:, :], sk_b[:, :], AF.Exp, scale=nhalf_col[:]), inc="ac")
                Sc.wait("vector", "ac", krst)
                for g in range(KVH):
                    Sc.op("vector", lambda v, g=g: v.tensor_scalar(
                        kvrow[:, g * 128:(g + 1) * 128], kvrow[:, g * 128:(g + 1) * 128],
                        sk_a[:, g:g + 1], None, ALU.mult), inc="ve")
                Sc.fence("vector", "ve")
                Sc.wait("vector", "sy", rs)
                Sc.wait("vector", "pe", M.get(("ktp", sb - 2)))
                k3 = kvrow.rearrange("p (g f) -> p g f", g=KVH)
                r3 = krot[:, sb % 2, :].rearrange("p (g f) -> p g f", g=KVH)
                c3 = kstage[:, 0:512].rearrange("p (g f) -> p g f", g=KVH)
                u3 = kstage[:, 512:1024].rearrange("p (g f) -> p g f", g=KVH)
                ct4 = ctile[:, 0:KVH, :]
                st4 = stile[:, 0:KVH, :]
                Sc.op("vector", lambda v: v.tensor_tensor(c3[:, :, 0:64], k3[:, :, 0:64], ct4, ALU.mult), inc="ve")
                Sc.op("vector", lambda v: v.tensor_tensor(u3[:, :, 0:64], k3[:, :, 64:128], st4, ALU.mult), inc="ve")
                Sc.fence("vector", "ve")
                Sc.op("vector", lambda v, r3=r3: v.tensor_tensor(r3[:, :, 0:64], c3[:, :, 0:64], u3[:, :, 0:64], ALU.add), inc="ve")
                Sc.op("vector", lambda v: v.tensor_tensor(c3[:, :, 64:128], k3[:, :, 64:128], ct4, ALU.mult), inc="ve")
                Sc.op("vector", lambda v: v.tensor_tensor(u3[:, :, 64:128], k3[:, :, 0:64], st4, ALU.mult), inc="ve")
                Sc.fence("vector", "ve")
                krd = Sc.op("vector", lambda v, r3=r3: v.tensor_tensor(r3[:, :, 64:128], c3[:, :, 64:128], u3[:, :, 64:128], ALU.subtract), inc="ve")
                M[("krd", sb)] = krd

            def emit_kT_tp(sb):
                Sc.wait("tensor", "ve", M[("krd", sb)])
                ktp, kte = emit_transposes(
                    krot[:, sb % 2, :], 4,
                    lambda g, cnt, sb=sb: kT_sb[:, g * 8:g * 8 + cnt, sb * 128:(sb + 1) * 128])
                M[("ktp", sb)] = ktp

            for sb in range(NSB):
                emit_kv_front(sb)
                if sb > 0:
                    emit_kT_tp(sb - 1)
            emit_kT_tp(NSB - 1)
            Sc.barrier()
            if dbg == "dk":
                Sc.dma("gpsimd", out[0:512, :].rearrange("(g p) s -> p g s", p=128), kT_sb[:, :, :], "gp")
                return
            if dbg == "dv":
                Sc.dma("gpsimd", out.rearrange("(sb p) e -> p sb e", p=128)[:, :, 0:512], v_sb[:, :, :], "gp")
                return

            wqv = M["wqv"]
            Sc.dma("sync", mask_sb[:], c_mask, "sy")
            Sc.wait("vector", "gp", M["wost"])
            Sc.barrier()
            for chunk in range(4):
                def emit_q_front(sbi, chunk=chunk):
                    sb = chunk * 4 + sbi
                    par = sb % 2
                    if sbi == 0:
                        Sc.wait("sync", "ve", Sc.current("ve"))
                        Sc.wait("vector", "pe", M.get(("ptp", chunk - 1, 63)))
                        Sc.wait("vector", "sy", M.get(("rst", chunk - 1, 1, 1)))
                    if sbi == 1 and chunk > 0:
                        Sc.wait("vector", "pe", M.get(("opmm", chunk - 1, 1, NHB - 1)))
                    emit_ln_tile(x1_sc, sb, "aq", stats_tag="a1")
                    Sc.wait("sync", "ve", M.get(("qrd", sb - 1)))
                    rc = Sc.dma("sync", ctile[:, :, :],
                                c_cos[sb * 128:(sb + 1) * 128, :, :], "sy")
                    rs = Sc.dma("sync", stile[:, :, :],
                                c_sin[sb * 128:(sb + 1) * 128, :, :], "sy")
                    Sc.wait("tensor", "ve", lnst["norm"][("aq", sb)])
                    emit_transposes(
                        xn_b, 16,
                        lambda g, cnt, par=par: xnT1c[:, par, g * 8:g * 8 + cnt, :])
                    if chunk == 0 and sbi == 0:
                        Sc.wait("tensor", "gp", wqv)
                    Sc.wait("tensor", "ac", M.get(("qev", sb - 2)))
                    for hb in range(NHB):
                        for n4 in range(4):
                            qmm = Sc.op("tensor", lambda t, par=par, hb=hb, n4=n4: t.matmul(
                                psA[:, n4 * 512:(n4 + 1) * 512], xnT1c[:, par, hb, :],
                                wq_sb[:, hb, n4 * 512:(n4 + 1) * 512],
                                start=(hb == 0), stop=(hb == NHB - 1)),
                                inc="pe" if (hb == NHB - 1 and n4 == 3) else None)
                    Sc.wait("scalar", "pe", qmm)
                    Sc.wait("scalar", "ve", M.get(("qap", sb - 1)))
                    qev = Sc.op("scalar", lambda s: s.activation(
                        xn_f[:], psA[:, :], AF.Identity), inc="ac")
                    M[("qev", sb)] = qev
                    Sc.wait("vector", "ac", qev)
                    Sc.op("vector", lambda v: v.tensor_tensor(
                        res_f[:, 0, :], xn_f[:], xn_f[:], ALU.mult), inc="ve")
                    Sc.fence("vector", "ve")
                    Sc.op("vector", lambda v: v.reduce_sum(
                        sq_a[:, :], res_f[:, 0, :].rearrange("p (g f) -> p g f", g=H),
                        axis=AX.X), inc="ve")
                    Sc.fence("vector", "ve")
                    qr1 = Sc.op("vector", lambda v: v.tensor_scalar(
                        sq_a[:, :], sq_a[:, :], 1.0 / HD, EPS_RMS, ALU.mult, ALU.add), inc="ve")
                    Sc.wait("scalar", "ve", qr1)
                    Sc.op("scalar", lambda s: s.activation(sq_b[:, :], sq_a[:, :], AF.Ln), inc="ac")
                    qrst = Sc.op("scalar", lambda s: s.activation(
                        sq_a[:, :], sq_b[:, :], AF.Exp, scale=nhalf_col[:]), inc="ac")
                    Sc.wait("vector", "ac", qrst)
                    fct = Sc.op("vector", lambda v: v.tensor_tensor(
                        sq_a[:, :], sq_a[:, :], gain_bc[:], ALU.mult), inc="ve")
                    Sc.wait("vector", "ve", fct)
                    for g in range(H):
                        Sc.op("vector", lambda v, g=g: v.tensor_scalar(
                            res_f[:, 1, g * 128:(g + 1) * 128], xn_f[:, g * 128:(g + 1) * 128],
                            sq_a[:, g:g + 1], None, ALU.mult), inc="ve")
                    qap = Sc.current("ve")
                    M[("qap", sb)] = qap
                    Sc.fence("vector", "ve")
                    Sc.wait("vector", "sy", rs)
                    Sc.wait("vector", "pe", M.get(("qtp", chunk * 4 + sbi - 2)))
                    rot = qrot if sbi % 2 == 0 else qrot2
                    q3 = res_f[:, 1, :].rearrange("p (g f) -> p g f", g=H)
                    o3 = rot.rearrange("p (g f) -> p g f", g=H)
                    c3q = res_f[:, 0, :].rearrange("p (g f) -> p g f", g=H)
                    w3 = qcross
                    Sc.op("vector", lambda v, q3=q3, c3q=c3q: v.tensor_tensor(
                        c3q[:, :, 0:64], q3[:, :, 0:64], ctile[:], ALU.mult), inc="ve")
                    Sc.op("vector", lambda v, q3=q3, w3=w3: v.tensor_tensor(
                        w3[:, :, 0:64], q3[:, :, 64:128], stile[:], ALU.mult), inc="ve")
                    Sc.fence("vector", "ve")
                    Sc.op("vector", lambda v, o3=o3, c3q=c3q, w3=w3: v.tensor_tensor(
                        o3[:, :, 0:64], c3q[:, :, 0:64], w3[:, :, 0:64], ALU.add), inc="ve")
                    Sc.op("vector", lambda v, q3=q3, c3q=c3q: v.tensor_tensor(
                        c3q[:, :, 64:128], q3[:, :, 64:128], ctile[:], ALU.mult), inc="ve")
                    Sc.op("vector", lambda v, q3=q3, w3=w3: v.tensor_tensor(
                        w3[:, :, 64:128], q3[:, :, 0:64], stile[:], ALU.mult), inc="ve")
                    Sc.fence("vector", "ve")
                    qrd = Sc.op("vector", lambda v, o3=o3, c3q=c3q, w3=w3: v.tensor_tensor(
                        o3[:, :, 64:128], c3q[:, :, 64:128], w3[:, :, 64:128], ALU.subtract), inc="ve")
                    M[("qrd", sb)] = qrd
                    lnst["xnf_free"][("aq", sb)] = ("ve", qap)

                def emit_qT_tp(sbi, chunk=chunk):
                    sb = chunk * 4 + sbi
                    rot = qrot if sbi % 2 == 0 else qrot2
                    Sc.wait("tensor", "ve", M[("qrd", sb)])
                    pev, _ = emit_transposes(
                        rot, 16,
                        lambda g, cnt, sbi=sbi: qT_sb[:, g * 8:g * 8 + cnt, sbi * 128:(sbi + 1) * 128])
                    M[("qtp", sb)] = pev

                for sbi in range(4):
                    emit_q_front(sbi)
                    if sbi > 0:
                        emit_qT_tp(sbi - 1)
                emit_qT_tp(3)
                Sc.barrier()
                if dbg == "dq":
                    Sc.dma("gpsimd", out[0:2048, chunk * 512:(chunk + 1) * 512].rearrange(
                        "(g p) s -> p g s", p=128), qT_sb[:, :, :], "gp")
                    if chunk == 3:
                        return
                    continue

                iters = []
                for qbi in range(4):
                    sb = chunk * 4 + qbi
                    s0 = sb * 128
                    t0 = max(0, s0 - 512)
                    w = s0 + 128 - t0
                    for kvh in range(KVH):
                        for qh in range(4):
                            iters.append((qbi, kvh, qh, t0, w))
                NG = len(iters)
                SLOTS = [psB[:, 0:640], psB[:, 1024:1664]]

                def emit_scores(g2):
                    qbi, kvh, qh, t0, w = iters[g2]
                    head = kvh * 4 + qh
                    sl = SLOTS[g2 % 2]
                    Sc.wait("tensor", "ac", M.get(("exp", chunk, g2 - 2)))
                    n1 = min(w, 512)
                    Sc.op("tensor", lambda t, sl=sl, head=head, qbi=qbi, kvh=kvh, t0=t0, n1=n1: t.matmul(
                        sl[:, 0:n1],
                        qT_sb[:, head, qbi * 128:(qbi + 1) * 128],
                        kT_sb[:, kvh, t0:t0 + n1],
                        start=True, stop=True),
                        inc="pe" if w <= 512 else None)
                    if w > 512:
                        Sc.op("tensor", lambda t, sl=sl, head=head, qbi=qbi, kvh=kvh, t0=t0, w=w: t.matmul(
                            sl[:, 512:w],
                            qT_sb[:, head, qbi * 128:(qbi + 1) * 128],
                            kT_sb[:, kvh, t0 + 512:t0 + w],
                            start=True, stop=True), inc="pe")
                    M[("scmm", chunk, g2)] = Sc.current("pe")

                def emit_stage1(g2):
                    qbi, kvh, qh, t0, w = iters[g2]
                    p3 = g2 % 3
                    sl = SLOTS[g2 % 2]
                    Sc.wait("vector", "pe", M[("scmm", chunk, g2)])
                    mkd = Sc.op("vector", lambda v, sl=sl, w=w: v.tensor_tensor(
                        sl[:, 0:w], sl[:, 0:w], mask_sb[:, 640 - w:640], ALU.add), inc="ve")
                    Sc.wait("scalar", "ve", mkd)
                    ex = Sc.op("scalar", lambda s, sl=sl, w=w, p3=p3: s.activation(
                        probs_f[:, p3, 0:w], sl[:, 0:w], AF.Exp,
                        accum_out=z_c[:, p3:p3 + 1]), inc="ac")
                    M[("exp", chunk, g2)] = ex

                def emit_stage2(g2):
                    qbi, kvh, qh, t0, w = iters[g2]
                    p3 = g2 % 3
                    Sc.wait("vector", "ac", M[("exp", chunk, g2)])
                    rzv = Sc.op("vector", lambda v, p3=p3: v.reciprocal(
                        rz_c[:, p3:p3 + 1], z_c[:, p3:p3 + 1]), inc="ve")
                    Sc.wait("vector", "ve", rzv)
                    Sc.wait("vector", "pe", M.get(("ptp", chunk, g2 - 3)))
                    psv = Sc.op("vector", lambda v, p3=p3, w=w: v.tensor_scalar(
                        probs_b[:, p3, 0:w], probs_f[:, p3, 0:w],
                        rz_c[:, p3:p3 + 1], None, ALU.mult), inc="ve")
                    M[("psv", chunk, g2)] = psv

                def emit_ptav(g2, grp):
                    qbi, kvh, qh, t0, w = iters[g2]
                    nb = w // 128
                    p3 = g2 % 3
                    pp = g2 % 2
                    Sc.wait("tensor", "ve", M[("psv", chunk, g2)])
                    Sc.wait("tensor", "ac", M.get(("aev", chunk, g2 - 1)))
                    for j in range(nb):
                        tpv = Sc.op("tensor", lambda t, p3=p3, j=j: t.matmul(
                            psA[:, 512 + j * 128: 512 + (j + 1) * 128],
                            probs_b[:, p3, j * 128:(j + 1) * 128], identb[:],
                            start=True, stop=True),
                            inc="pe" if j == nb - 1 else None)
                    M[("ptp", chunk, g2)] = tpv
                    Sc.wait("scalar", "pe", tpv)
                    Sc.wait("scalar", "pe", M.get(("avv", chunk, g2 - 2)))
                    Sc.op("scalar", lambda s, pp=pp, nb=nb: s.activation(
                        attnT_b[:, pp, 0:nb, :],
                        psA[:, 512: 512 + nb * 128].rearrange(
                            "p (j f) -> p j f", j=nb), AF.Identity), inc="ac")
                    M[("aev", chunk, g2)] = Sc.current("ac")
                    Sc.wait("tensor", "ac", M[("aev", chunk, g2)])
                    if qh == 0:
                        Sc.wait("tensor", "ve", M.get(("yev", chunk, grp - 1)))
                    avv = None
                    for tb in range(nb):
                        avv = Sc.op("tensor", lambda t, qh=qh, kvh=kvh, t0=t0, tb=tb, pp=pp: t.matmul(
                            psA[:, qh * 128:(qh + 1) * 128],
                            v_sb[:, t0 // 128 + tb, kvh * 128:(kvh + 1) * 128],
                            attnT_b[:, pp, tb, :],
                            start=(tb == 0), stop=(tb == nb - 1)),
                            inc="pe" if tb == nb - 1 else None)
                    M[("avv", chunk, g2)] = avv
                    if qh == 3:
                        Sc.wait("vector", "pe", avv)
                        M[("yev", chunk, grp)] = Sc.op(
                            "vector", lambda v, kvh=kvh, qbi=qbi: v.tensor_copy(
                                yT_sb[:, kvh * 4:(kvh + 1) * 4, qbi * 128:(qbi + 1) * 128],
                                psA[:, 0:512].rearrange("p (j f) -> p j f", j=4)), inc="ve")

                emit_scores(0)
                emit_stage1(0)
                emit_scores(1)
                for g2 in range(NG):
                    emit_stage2(g2)
                    if g2 + 1 < NG:
                        emit_stage1(g2 + 1)
                    if g2 + 2 < NG:
                        emit_scores(g2 + 2)
                    emit_ptav(g2, g2 // 4)
                Sc.wait("tensor", "ac", Sc.current("ac"))
                Sc.wait("tensor", "ve", Sc.current("ve"))
                Sc.wait("sync", "ve", Sc.current("ve"))
                if dbg == "dy":
                    Sc.dma("gpsimd", out[0:2048, chunk * 512:(chunk + 1) * 512].rearrange(
                        "(g p) s -> p g s", p=128), yT_sb[:, :, :], "gp")
                    if chunk == 3:
                        return
                    continue

                for qbp in range(2):
                    for db in range(NHB):
                        wpar = db % 2
                        Sc.wait("sync", "pe", M.get(("opmm", chunk, qbp, db - 2)))
                        if db < 2 and qbp > 0:
                            Sc.wait("sync", "pe", M.get(("opmm", chunk, qbp - 1, 14 + db)))
                        wld = Sc.dma("sync", wo_row[:, wpar, :], wo_sc[db], "sy")
                        Sc.wait("tensor", "sy", wld)
                        if db == 0 and qbp > 0:
                            Sc.wait("tensor", "ve", M.get(("orv", chunk, qbp - 1, 1)))
                        for j in range(2):
                            ps = psA if j == 0 else psB
                            qb = qbp * 2 + j
                            for n4 in range(4):
                                opv = Sc.op("tensor", lambda t, ps=ps, db=db, qb=qb, n4=n4, wpar=wpar: t.matmul(
                                    ps[:, n4 * 512:(n4 + 1) * 512],
                                    yT_sb[:, db, qb * 128:(qb + 1) * 128],
                                    wo_row[:, wpar, n4 * 512:(n4 + 1) * 512],
                                    start=(db == 0), stop=(db == NHB - 1)),
                                    inc="pe" if (j == 1 and n4 == 3) else None)
                        M[("opmm", chunk, qbp, db)] = Sc.current("pe")
                    for j in range(2):
                        ps = psA if j == 0 else psB
                        sb = chunk * 4 + qbp * 2 + j
                        Sc.wait("sync", "ve", M.get(("orv", chunk, qbp - 1, j)))
                        ld = Sc.dma("sync", xt[:, j, :], x1_sc[sb * 128:(sb + 1) * 128, :], "sy")
                        Sc.wait("vector", "pe", M[("opmm", chunk, qbp, NHB - 1)])
                        Sc.wait("vector", "sy", ld)
                        Sc.wait("vector", "sy", M.get(("rst", chunk, qbp - 1, j)))
                        rv = Sc.op("vector", lambda v, ps=ps, j=j: v.tensor_tensor(
                            res_f[:, j, :], ps[:, :], xt[:, j, :], ALU.add), inc="ve")
                        M[("orv", chunk, qbp, j)] = rv
                        Sc.wait("sync", "ve", rv)
                        M[("rst", chunk, qbp, j)] = Sc.dma(
                            "sync", attn_dst[sb * 128:(sb + 1) * 128, :], res_f[:, j, :], "sy")
            Sc.barrier()

        if "dln" in parts:
            for sb in range(NSB):
                par, norm = emit_ln_tile_chained(x0, sb, "dln")
                Sc.wait("gpsimd", "ve", norm)
                stv = Sc.dma("gpsimd", out[sb * 128:(sb + 1) * 128, :], xn_b[:], "gp")
                lnst["xnf_free"][("dln", sb)] = None
                Sc.wait("scalar", "gp", stv)
        if "dxnt" in parts:
            for sb in range(NSB):
                emit_ln_tile_chained(x0, sb, "dxnt")
                Sc.wait("tensor", "ve", lnst["norm"][("dxnt", sb)])
                pev, _ = emit_transposes(
                    xn_b, 16,
                    lambda g, cnt, sb=sb: xnT[:, g * 8:g * 8 + cnt, sb * 128:(sb + 1) * 128])
                lnst["xnf_free"][("dxnt", sb)] = ("pe", pev)
            Sc.barrier()
            Sc.dma("gpsimd", out.rearrange("(hb p) s -> p hb s", p=128), xnT[:, :, :], "gp")
        at_dbg_probe = any(q in parts for q in ("dk", "dv", "dq", "dy"))
        if "f1" in parts:
            use_x1 = ("at" in parts or "f2" in parts or at_dbg_probe)
            emit_ffn(1, x0, x1_sc if use_x1 else out,
                     a1_stats=("at" in parts or at_dbg_probe))
        at_dbg = None
        for pp_ in parts:
            if pp_.startswith("d") and pp_ in ("dk", "dv", "dq", "dy"):
                at_dbg = pp_
        if "at" in parts or at_dbg:
            emit_attn(x2_sc if "f2" in parts else out, dbg=at_dbg)
        if "f2" in parts:
            emit_ffn(2, x2_sc if "at" in parts else x1_sc, out)

        Sc.wait("sync", "sy", Sc.current("sy"))
        Sc.wait("sync", "sq", Sc.current("sq"))
        Sc.wait("sync", "gp", Sc.current("gp"))

        Sc.check()
        blk = es.enter_context(nc.Block())
        Sc.emit(blk, sem_handles)
    return nc


def kernel(L0_attn_x, L0_attn_Wq, L0_attn_Wk, L0_attn_Wv, L0_attn_Wo,
           L0_attn_q_gain, L0_ffn_gate_w, L0_ffn_gate_scale, L0_ffn_up_w,
           L0_ffn_up_scale, L0_ffn_down_w, L0_ffn_down_scale):
    from concourse.bass_utils import run_bass_kernel_spmd

    x = np.ascontiguousarray(np.asarray(L0_attn_x, dtype=np.float32))
    B = x.shape[0]

    half = 64
    inv_freq = (10000.0 ** (-(np.arange(0, half, dtype=np.float32)) / half)).astype(np.float32)
    freqs = np.arange(S, dtype=np.float32)[:, None] * inv_freq[None, :]
    c_cos = np.ascontiguousarray(
        np.broadcast_to(np.cos(freqs).astype(np.float32)[:, None, :], (S, H, half)))
    c_sin = np.ascontiguousarray(
        np.broadcast_to(np.sin(freqs).astype(np.float32)[:, None, :], (S, H, half)))
    mask = np.full((128, 640), -1e30, dtype=np.float32)
    for i in range(128):
        mask[i, i + 1:i + 513] = 0.0

    shared = {
        "wq_t": np.ascontiguousarray(np.asarray(L0_attn_Wq, dtype=np.float32).T),
        "wk_t": np.ascontiguousarray(np.asarray(L0_attn_Wk, dtype=np.float32).T),
        "wv_t": np.ascontiguousarray(np.asarray(L0_attn_Wv, dtype=np.float32).T),
        "wo_t": np.ascontiguousarray(np.asarray(L0_attn_Wo, dtype=np.float32).T),
        "q_gain": np.ascontiguousarray(np.asarray(L0_attn_q_gain, dtype=np.float32).reshape(1, H)),
        "gw_t": np.ascontiguousarray(np.asarray(L0_ffn_gate_w, dtype=np.int32).T),
        "uw_t": np.ascontiguousarray(np.asarray(L0_ffn_up_w, dtype=np.int32).T),
        "dw_t": np.ascontiguousarray(np.asarray(L0_ffn_down_w, dtype=np.int32).T),
        "g_sc": np.ascontiguousarray(np.asarray(L0_ffn_gate_scale, dtype=np.float32).reshape(NIB, 128).T),
        "u_sc": np.ascontiguousarray(np.asarray(L0_ffn_up_scale, dtype=np.float32).reshape(NIB, 128).T),
        "d_sc": np.ascontiguousarray(np.asarray(L0_ffn_down_scale, dtype=np.float32).reshape(1, D)),
        "c_ident": np.eye(128, dtype=np.float32),
        "c_cos": c_cos,
        "c_sin": c_sin,
        "c_mask": mask,
    }
    in_maps = [dict(shared, x0=np.ascontiguousarray(x[c])) for c in range(B)]

    nc = build_program(parts=("f1", "at", "f2"))
    res = run_bass_kernel_spmd(nc, in_maps, core_ids=list(range(B)))
    outs = [res.results[c]["out"] for c in range(B)]
    return np.stack(outs, axis=0).astype(np.float32)
